# version 17
# baseline (speedup 1.0000x reference)
import contextlib

import numpy as np

import concourse.bass as bass
import concourse.mybir as mybir
from concourse.alu_op_type import AluOpType as OP
from concourse.bass_utils import run_bass_kernel_spmd
from concourse.replica_groups import maybe_share_collective_output_space

F32 = mybir.dt.float32
BF16 = mybir.dt.bfloat16
AF = mybir.ActivationFunctionType
AX = mybir.AxisListType.X

NCORE = 8
D = 1024
DI = 2048
NS = 16
RK = 64
XD = 96
I3 = 3072
QM = 32767.0
EPS = 1e-5
M32 = 12582912.0
M52 = 4503599627370496.0
M45 = 35184372088832.0
CONST_VALS = [1e-5, 1.0, 0.5, -1.0, -0.5, 65536.0, 2.0, 0.25]
KD = D // 128
KDI = DI // 128
KI3 = I3 // 128


class Tok:
    __slots__ = ("sem", "cnt")

    def __init__(self, sem, cnt):
        self.sem = sem
        self.cnt = cnt


class Buf:
    __slots__ = ("w", "r")

    def __init__(self):
        self.w = None
        self.r = []


class DmaSem:
    __slots__ = ("h", "cnt")

    def __init__(self, h):
        self.h = h
        self.cnt = 0


class Eng:
    MAX_PENDING = 8000

    def __init__(self, section, sem):
        self.section = section
        self.sem = sem
        self.cnt = 0
        self.waited = {}
        self.pending = []

    def flush(self):
        if not self.pending:
            return
        p = self.pending
        self.pending = []

        def body(eng):
            for r in p:
                r(eng)

        self.section(body)

    def _collect(self, ins, outs, deps):
        toks = []
        for b in ins:
            if b.w is not None:
                toks.append(b.w)
        for b in outs:
            if b.w is not None:
                toks.append(b.w)
            toks.extend(b.r)
        for t in deps:
            if t is not None:
                toks.append(t)
        need = {}
        for t in toks:
            k = id(t.sem)
            if self.waited.get(k, 0) < t.cnt and need.get(k, (None, 0))[1] < t.cnt:
                need[k] = (t.sem, t.cnt)
        for k, (s, c) in need.items():
            self.waited[k] = c
        return list(need.values())

    def emit(self, fn, ins=(), outs=(), deps=(), dma=None):
        return self.emit_multi(fn, ins, outs, deps, dma)

    def emit_multi(self, body, ins=(), outs=(), deps=(), dma=None):
        need = self._collect(ins, outs, deps)
        if dma is None:
            self.cnt += 1
            sem_h, cnt, step = self.sem, self.cnt, 1
        else:
            dma.cnt += 16
            sem_h, cnt, step = dma.h, dma.cnt, 16

        def run(eng):
            for s, c in need:
                eng.wait_ge(s, c)
            body(eng).then_inc(sem_h, step)

        self.pending.append(run)
        if len(self.pending) >= self.MAX_PENDING:
            self.flush()
        tok = Tok(sem_h, cnt)
        for b in ins:
            b.r.append(tok)
        for b in outs:
            b.w = tok
            b.r = []
        return tok


def build_nc(L, T, upto="all", ncc=NCORE):
    NT = L // T
    JT = T // 128
    LP = L // 128
    nc = bass.Bass("TRN2", target_bir_lowering=False)

    x_in = nc.dram_tensor("x_in", [L, D], F32, kind="ExternalInput")
    c_in = nc.dram_tensor("c_in", [1, D], F32, kind="ExternalInput")
    w_in_T = nc.dram_tensor("w_in_T", [D, 2 * DI], F32, kind="ExternalInput")
    conv_w_in = nc.dram_tensor("conv_w_in", [DI, 4], F32, kind="ExternalInput")
    conv_b_in = nc.dram_tensor("conv_b_in", [DI], F32, kind="ExternalInput")
    w_xproj_T = nc.dram_tensor("w_xproj_T", [DI, XD], F32, kind="ExternalInput")
    w_dt_T = nc.dram_tensor("w_dt_T", [RK, DI], F32, kind="ExternalInput")
    dt_b_in = nc.dram_tensor("dt_b_in", [DI], F32, kind="ExternalInput")
    a_log_in = nc.dram_tensor("a_log_in", [DI, NS], F32, kind="ExternalInput")
    d_par_in = nc.dram_tensor("d_par_in", [DI], F32, kind="ExternalInput")
    w_out_T = nc.dram_tensor("w_out_T", [DI, D], F32, kind="ExternalInput")
    w_gate_in = nc.dram_tensor("w_gate_in", [I3, D], F32, kind="ExternalInput")
    w_up_in = nc.dram_tensor("w_up_in", [I3, D], F32, kind="ExternalInput")
    w_down_in = nc.dram_tensor("w_down_in", [D, I3], F32, kind="ExternalInput")
    wmod_in = nc.dram_tensor("wmod_in", [D, 6 * D], F32, kind="ExternalInput")
    eye_in = nc.dram_tensor("eye_in", [128, 128], F32, kind="ExternalInput")
    sel32_in = nc.dram_tensor("sel32_in", [32, 32 * 128], BF16, kind="ExternalInput")
    sel4_in = nc.dram_tensor("sel4_in", [4, 4 * 128], F32, kind="ExternalInput")
    consts_in = nc.dram_tensor("consts_in", [128, 20], F32, kind="ExternalInput")
    out_t = nc.dram_tensor("out", [L, D], F32, kind="ExternalOutput")

    x_fm_d = nc.dram_tensor("x_fm_d", [KD, 128, L], F32)
    h_hi_d = nc.dram_tensor("h_hi_d", [KD, 128, L], BF16)
    h_lo_d = nc.dram_tensor("h_lo_d", [KD, 128, L], BF16)
    z_d = nc.dram_tensor("z_d", [KDI, 128, L], F32)
    u_d = nc.dram_tensor("u_d", [KDI, 128, L], F32)
    y_d = nc.dram_tensor("y_d", [KDI, 128, L], F32)
    ygh_d = nc.dram_tensor("ygh_d", [KDI, 128, L], BF16)
    ygl_d = nc.dram_tensor("ygl_d", [KDI, 128, L], BF16)
    xdbl_d = nc.dram_tensor("xdbl_d", [XD, L], F32)
    x2_d = nc.dram_tensor("x2_d", [KD, 128, L], F32)
    k1h_d = nc.dram_tensor("k1h_d", [KD, 128, L], BF16)
    k1l_d = nc.dram_tensor("k1l_d", [KD, 128, L], BF16)
    upraw_d = nc.dram_tensor("upraw_d", [KI3, 128, L], F32)
    gateraw_d = nc.dram_tensor("gateraw_d", [KI3, 128, L], F32)
    g2raw_d = nc.dram_tensor("g2raw_d", [KI3, 128, L], F32)
    oraw_d = nc.dram_tensor("oraw_d", [KI3, 128, L], F32)
    roh_d = nc.dram_tensor("roh_d", [KI3, 128, L], BF16)
    rol_d = nc.dram_tensor("rol_d", [KI3, 128, L], BF16)
    rows_d = nc.dram_tensor("rows_d", [8, L], F32)
    mod_d = nc.dram_tensor("mod_d", [1, 6 * D], F32)
    colb_d = nc.dram_tensor("colb_d", [128, 1], F32)
    core_ids = list(range(ncc))
    spc = maybe_share_collective_output_space("AllReduce", [core_ids])
    cc_i = {k: nc.dram_tensor("cc_i%s" % k, [1, w], F32)
            for k, w in (("1", 1), ("23", 2), ("4", 1), ("5", 1))}
    cc_o = {k: nc.dram_tensor("cc_o%s" % k, [1, w], F32, addr_space=spc)
            for k, w in (("1", 1), ("23", 2), ("4", 1), ("5", 1))}

    es = contextlib.ExitStack()

    def sbt(name, shape, dt=F32):
        return es.enter_context(nc.sbuf_tensor(name, shape, dt))

    eye = sbt("eye", [128, 128])
    consts = sbt("consts", [128, 20])
    sel4 = sbt("sel4", [4, 4 * 128])
    sel32 = sbt("sel32", [32, 32 * 128], BF16)
    ones_row = sbt("ones_row", [1, 128])
    ones_col = sbt("ones_col", [128, 1])
    a_neg = sbt("a_neg", [128, KDI * NS])
    conv_w = sbt("conv_w", [128, KDI * 4])
    conv_b = sbt("conv_b", [128, KDI])
    dt_b = sbt("dt_b", [128, KDI])
    d_par = sbt("d_par", [128, KDI])
    dtw_h = sbt("dtw_h", [RK, DI], BF16)
    dtw_l = sbt("dtw_l", [RK, DI], BF16)
    xpj_h = sbt("xpj_h", [128, KDI * XD], BF16)
    xpj_l = sbt("xpj_l", [128, KDI * XD], BF16)
    modc = sbt("modc", [128, 6 * KD])
    modc2 = sbt("modc2", [128, 3 * KD])
    rcols = sbt("rcols", [128, 16])
    stat_st = sbt("stat_st", [4, T])
    bc_h = sbt("bc_h", [32, T], BF16)
    bc_l = sbt("bc_l", [32, T], BF16)
    bc_f = sbt("bc_f", [32, T])
    row_st = sbt("row_st", [1, T])
    row_st2 = sbt("row_st2", [1, T])
    r128 = sbt("r128", [1, 128])
    r1t = sbt("r1t", [1, 2])
    s64a = sbt("s64a", [128, 4 * LP])
    halo = sbt("halo", [128, KDI * 4])
    mcol = sbt("mcol", [128, 8])

    FW = 20500
    BW = 33792
    f32a = sbt("f32a", [128, FW])
    bfa = sbt("bfa", [128, BW], BF16)

    pbank = [es.enter_context(nc.psum_tensor("pb%d" % i, [128, 512], F32))
             for i in range(8)]

    esem = {n: es.enter_context(nc.semaphore(n)) for n in
            ["sv", "ss", "st", "sg", "ssp", "bar"]}
    dpool = [DmaSem(es.enter_context(nc.semaphore("dm%d" % i))) for i in range(48)]

    blk = es.enter_context(nc.Block())
    V = Eng(blk.vector, esem["sv"])
    S = Eng(blk.scalar, esem["ss"])
    TE = Eng(blk.tensor, esem["st"])
    G = Eng(blk.gpsimd, esem["sg"])
    SP = Eng(blk.sync, esem["ssp"])
    ENGS = [V, S, TE, G, SP]

    BUFS = {}

    def bf(key):
        if key not in BUFS:
            BUFS[key] = Buf()
        return BUFS[key]

    _di = [0]

    def dsem():
        s = dpool[_di[0] % len(dpool)]
        _di[0] += 1
        return s

    def load(dst, src, outs, ins=(), deps=(), slow=False):
        return SP.emit(lambda e: e.dma_start(out=dst, in_=src,
                                             allow_slow_non_contiguous=slow),
                       ins=ins, outs=outs, deps=deps, dma=dsem())

    def store(dst, src, ins, outs=(), deps=(), accum=False):
        if accum:
            return SP.emit(lambda e: e.dma_start(out=dst, in_=src, accum_op=OP.add),
                           ins=ins, outs=outs, deps=deps, dma=dsem())
        return SP.emit(lambda e: e.dma_start(out=dst, in_=src),
                       ins=ins, outs=outs, deps=deps, dma=dsem())

    def fsl(off, w, p=128, p0=0):
        assert off + w <= FW, (off, w)
        return f32a[p0:p0 + p, off:off + w]

    def bsl(off, w, p=128, p0=0):
        assert off + w <= BW, (off, w)
        return bfa[p0:p0 + p, off:off + w]

    _bar = [0]

    def barrier():
        for e in ENGS:
            e.flush()
        waits = [(e.sem, e.cnt) for e in ENGS] + [(d.h, d.cnt) for d in dpool]
        waits = [(s, c) for (s, c) in waits if c > 0]
        _bar[0] += 1
        bval = _bar[0]

        def spf(e):
            for s, c in waits:
                e.wait_ge(s, c)
            e.nop().then_inc(esem["bar"], 1)

        blk.sync(spf)
        for eng in (V, S, TE, G):
            eng.section(lambda e: e.wait_ge(esem["bar"], bval))
        for eng in ENGS:
            for s, c in waits:
                eng.waited[id(s)] = max(eng.waited.get(id(s), 0), c)
            eng.waited[id(esem["bar"])] = bval

    def split_pair(src_ap, hi_ap, lo_ap, src_b, hi_b, lo_b, eng=None):
        eng = eng or V
        eng.emit(lambda e: e.tensor_copy(hi_ap, src_ap), ins=[src_b], outs=[hi_b])
        eng.emit(lambda e: e.tensor_tensor(lo_ap, src_ap, hi_ap, OP.subtract),
                 ins=[src_b, hi_b], outs=[lo_b])

    def rne(y_ap, y_b, r_ap, r_b, t1_ap, t1_b, t2_ap, t2_b):
        return rne_e(V, y_ap, y_b, r_ap, r_b, t1_ap, t1_b, t2_ap, t2_b)

    def rne_e(E, y_ap, y_b, r_ap, r_b, t1_ap, t1_b, t2_ap, t2_b):
        E.emit(lambda e: e.tensor_scalar(t1_ap, y_ap, M52, -M52, OP.add, OP.add),
               ins=[y_b], outs=[t1_b])
        E.emit(lambda e: e.tensor_tensor(t2_ap, y_ap, t1_ap, OP.subtract),
               ins=[y_b, t1_b], outs=[t2_b])
        E.emit(lambda e: e.tensor_scalar(r_ap, t2_ap, M32, -M32, OP.add, OP.add),
               ins=[t2_b], outs=[r_b])
        E.emit(lambda e: e.tensor_scalar(t2_ap, t2_ap, M45, -M45, OP.add, OP.add),
               ins=[t2_b], outs=[t2_b])
        E.emit(lambda e: e.tensor_tensor(r_ap, r_ap, t2_ap, OP.subtract),
               ins=[r_b, t2_b], outs=[r_b])
        E.emit(lambda e: e.tensor_tensor(r_ap, r_ap, t1_ap, OP.add),
               ins=[r_b, t1_b], outs=[r_b])
        return r_b

    def split256(r_ap, r_b, hi_ap, hi_b, lo_ap, lo_b, t_ap, t_b):
        return split256_e(V, r_ap, r_b, hi_ap, hi_b, lo_ap, lo_b, t_ap, t_b)

    def split256_e(E, r_ap, r_b, hi_ap, hi_b, lo_ap, lo_b, t_ap, t_b):
        E.emit(lambda e: e.tensor_scalar(t_ap, r_ap, 0.00390625, M32,
                                         OP.mult, OP.add),
               ins=[r_b], outs=[t_b])
        E.emit(lambda e: e.tensor_scalar(hi_ap, t_ap, -M32, 256.0,
                                         OP.add, OP.mult),
               ins=[t_b], outs=[hi_b])
        E.emit(lambda e: e.tensor_tensor(lo_ap, r_ap, hi_ap, OP.subtract),
               ins=[r_b, hi_b], outs=[lo_b])

    def gtt(dst, a, b, op, ins, outs):
        G.emit(lambda e: e.tensor_tensor(dst, a, b, op), ins=ins, outs=outs)

    def bctt(dst, a, col, op, ins, outs, w):
        d3 = dst.rearrange("p (c t) -> p c t", t=1)
        a3 = a.rearrange("p (c t) -> p c t", t=1)
        b3 = col.unsqueeze(1).broadcast_to([128, w, 1])
        G.emit(lambda e: e.tensor_tensor(d3, a3, b3, op), ins=ins, outs=outs)

    def rne_g(y_ap, y_b, r_ap, r_b, t1_ap, t1_b, t2_ap, t2_b, w):
        cb_ = bf("consts")
        cj = lambda j: consts[:, j:j + 1]
        bctt(t1_ap, y_ap, cj(8), OP.add, [y_b, cb_], [t1_b], w)
        bctt(t1_ap, t1_ap, cj(9), OP.add, [t1_b, cb_], [t1_b], w)
        gtt(t2_ap, y_ap, t1_ap, OP.subtract, [y_b, t1_b], [t2_b])
        bctt(r_ap, t2_ap, cj(10), OP.add, [t2_b, cb_], [r_b], w)
        bctt(r_ap, r_ap, cj(11), OP.add, [r_b, cb_], [r_b], w)
        bctt(t2_ap, t2_ap, cj(12), OP.add, [t2_b, cb_], [t2_b], w)
        bctt(t2_ap, t2_ap, cj(13), OP.add, [t2_b, cb_], [t2_b], w)
        gtt(r_ap, r_ap, t2_ap, OP.subtract, [r_b, t2_b], [r_b])
        gtt(r_ap, r_ap, t1_ap, OP.add, [r_b, t1_b], [r_b])
        return r_b

    def split256_g(r_ap, r_b, hi_ap, hi_b, lo_ap, lo_b, t_ap, t_b, w):
        cb_ = bf("consts")
        cj = lambda j: consts[:, j:j + 1]
        bctt(t_ap, r_ap, cj(15), OP.mult, [r_b, cb_], [t_b], w)
        bctt(t_ap, t_ap, cj(10), OP.add, [t_b, cb_], [t_b], w)
        bctt(t_ap, t_ap, cj(11), OP.add, [t_b, cb_], [t_b], w)
        bctt(hi_ap, t_ap, cj(16), OP.mult, [t_b, cb_], [hi_b], w)
        gtt(lo_ap, r_ap, hi_ap, OP.subtract, [r_b, hi_b], [lo_b])

    def mmchain(bank_key, mms, dep_bufs):
        bank = bf(bank_key)

        def body(e):
            last = None
            for (pa, lh, rh, st_, sp_) in mms:
                last = e.matmul(pa, lh, rh, start=st_, stop=sp_)
            return last

        return TE.emit_multi(body, ins=list(dep_bufs), outs=[bank])

    def trchain(bank_key, trs, dep_bufs):
        bank = bf(bank_key)

        def body(e):
            last = None
            for (pa, ia) in trs:
                last = e.transpose(pa, ia, eye[:, :])
            return last

        return TE.emit_multi(body, ins=list(dep_bufs) + [bf("eye")], outs=[bank])

    load(eye[:, :], eye_in.ap(), outs=[bf("eye")])
    load(consts[:, :], consts_in.ap(), outs=[bf("consts")])
    for j, cv in enumerate(CONST_VALS):
        nc.const_aps.register(F32, cv, consts[:, j:j + 1])
    load(sel4[:, :], sel4_in.ap(), outs=[bf("sel4")])
    load(sel32[:, :], sel32_in.ap(), outs=[bf("sel32")])
    V.emit(lambda e: e.memset(ones_row[:, :], 1.0), outs=[bf("ones_row")])
    V.emit(lambda e: e.memset(ones_col[:, :], 1.0), outs=[bf("ones_col")])
    htok = V.emit(lambda e: e.memset(halo[:, :], 0.0), outs=[bf("halo_init")])
    V.emit(lambda e: e.memset(stat_st[:, :], 0.0), outs=[bf("stat_st")])
    for hcc in range(KDI):
        bf(("halo", hcc)).w = htok
    load(a_neg[:, :], a_log_in.ap().rearrange("(c p) n -> p c n", p=128),
         outs=[bf("a_neg")])
    S.emit(lambda e: e.activation(a_neg[:, :], a_neg[:, :], AF.Exp),
           ins=[bf("a_neg")], outs=[bf("a_neg")])
    V.emit(lambda e: e.tensor_scalar_mul(a_neg[:, :], a_neg[:, :], -1.0),
           ins=[bf("a_neg")], outs=[bf("a_neg")])
    load(conv_w[:, :], conv_w_in.ap().rearrange("(c p) w -> p c w", p=128),
         outs=[bf("conv_w")])
    load(conv_b[:, :], conv_b_in.ap().rearrange("(c p) -> p c", p=128),
         outs=[bf("conv_b")], slow=True)
    load(dt_b[:, :], dt_b_in.ap().rearrange("(c p) -> p c", p=128),
         outs=[bf("dt_b")], slow=True)
    load(d_par[:, :], d_par_in.ap().rearrange("(c p) -> p c", p=128),
         outs=[bf("d_par")], slow=True)
    dtw_f = fsl(0, DI, p=RK)
    load(dtw_f, w_dt_T.ap(), outs=[bf("dtw_f")])
    split_pair(dtw_f, dtw_h[:, :], dtw_l[:, :], bf("dtw_f"), bf("dtw_h"), bf("dtw_l"))
    xpj_f = fsl(2048, KDI * XD)
    load(xpj_f, w_xproj_T.ap().rearrange("(k p) o -> p k o", p=128),
         outs=[bf("xpj_f")])
    split_pair(xpj_f, xpj_h[:, :], xpj_l[:, :], bf("xpj_f"), bf("xpj_h"), bf("xpj_l"))

    c_fm = fsl(4000, KD)
    load(c_fm, c_in.ap().rearrange("o (k p) -> p o k", p=128), outs=[bf("c_fm")], slow=True)
    c_sg = fsl(4100, KD)
    S.emit(lambda e: e.activation(c_sg, c_fm, AF.Sigmoid), ins=[bf("c_fm")],
           outs=[bf("c_sg")])
    c_sl = fsl(4200, KD)
    V.emit(lambda e: e.tensor_tensor(c_sl, c_fm, c_sg, OP.mult),
           ins=[bf("c_fm"), bf("c_sg")], outs=[bf("c_sl")])
    sc_h, sc_l = bsl(0, KD), bsl(KD, KD)
    split_pair(c_sl, sc_h, sc_l, bf("c_sl"), bf("sc_h"), bf("sc_l"))
    WMW = 6 * D
    wmf = fsl(4400, WMW)
    modrow = fsl(11000, WMW, p=1)
    wm_h = bsl(2 * KD, WMW)
    wm_l = bsl(2 * KD + WMW, WMW)
    for k in range(KD):
        load(wmf, wmod_in.ap()[k * 128:(k + 1) * 128, :], outs=[bf("wmf")])
        split_pair(wmf, wm_h, wm_l, bf("wmf"), bf("wm_h"), bf("wm_l"))
        for piece in range(WMW // 512):
            mms = [(pbank[0][0:1, :], sc_h[:, k:k + 1],
                    wm_h[:, piece * 512:(piece + 1) * 512], True, False),
                   (pbank[0][0:1, :], sc_h[:, k:k + 1],
                    wm_l[:, piece * 512:(piece + 1) * 512], False, False),
                   (pbank[0][0:1, :], sc_l[:, k:k + 1],
                    wm_h[:, piece * 512:(piece + 1) * 512], False, True)]
            mmchain(("pb", 0), mms, [bf("sc_h"), bf("sc_l"), bf("wm_h"), bf("wm_l")])
            mr = modrow[:, piece * 512:(piece + 1) * 512]
            if k == 0:
                V.emit(lambda e, mr=mr: e.tensor_copy(mr, pbank[0][0:1, :]),
                       ins=[bf(("pb", 0))], outs=[bf("modrow")])
            else:
                V.emit(lambda e, mr=mr: e.tensor_tensor(mr, mr, pbank[0][0:1, :],
                                                        OP.add),
                       ins=[bf(("pb", 0)), bf("modrow")], outs=[bf("modrow")])
    store(mod_d.ap(), modrow, ins=[bf("modrow")], outs=[bf("mod_d")])
    load(modc[:, :], mod_d.ap().rearrange("o (v k p) -> p o v k", v=6, p=128),
         outs=[bf("modc")], ins=[bf("mod_d")], slow=True)
    for v in (1, 4):
        V.emit(lambda e, v=v: e.tensor_scalar_add(modc[:, v * KD:(v + 1) * KD],
                                                  modc[:, v * KD:(v + 1) * KD], 1.0),
               ins=[bf("modc")], outs=[bf("modc")])
    gtmp = fsl(17200, KD)
    gtmp2 = fsl(17300, KD)
    for v in (2, 5):
        gsl = modc[:, v * KD:(v + 1) * KD]
        S.emit(lambda e, g=gsl: e.activation(gtmp, g, AF.Sign),
               ins=[bf("modc")], outs=[bf("gtmp")])
        S.emit(lambda e: e.activation(gtmp2, gtmp, AF.Abs),
               ins=[bf("gtmp")], outs=[bf("gtmp2")])
        V.emit(lambda e: e.tensor_scalar(gtmp2, gtmp2, -1.0, 1.0, OP.mult, OP.add),
               ins=[bf("gtmp2")], outs=[bf("gtmp2")])
        V.emit(lambda e: e.tensor_tensor(gtmp, gtmp, gtmp2, OP.add),
               ins=[bf("gtmp"), bf("gtmp2")], outs=[bf("gtmp")])
        S.emit(lambda e, g=gsl: e.activation(gtmp2, g, AF.Abs),
               ins=[bf("modc")], outs=[bf("gtmp2")])
        V.emit(lambda e: e.tensor_single_scalar(gtmp2, gtmp2, 0.25, OP.max),
               ins=[bf("gtmp2")], outs=[bf("gtmp2")])
        V.emit(lambda e, g=gsl: e.tensor_tensor(g, gtmp, gtmp2, OP.mult),
               ins=[bf("gtmp"), bf("gtmp2")], outs=[bf("modc")])

    barrier()
    if upto == "setup":
        es.close()
        return nc

    def p_phase():
        xtm = [fsl(0, JT * D), fsl(JT * D, JT * D)]
        xfm_st = fsl(2 * JT * D, KD * T)
        sqt = fsl(2 * JT * D + KD * T, T)
        for tc in range(NT):
            sl = tc % 2
            load(xtm[sl], x_in.ap()[tc * T:(tc + 1) * T, :]
                 .rearrange("(j p) f -> p j f", p=128), outs=[bf(("xtm", sl))])
            for k in range(KD):
                bk = k % 4
                trchain(("pb", bk),
                        [(pbank[bk][:, j * 128:(j + 1) * 128],
                          xtm[sl][:, j * D + k * 128:j * D + k * 128 + 128])
                         for j in range(JT)],
                        [bf(("xtm", sl))])
                xc = xfm_st[:, k * T:(k + 1) * T]
                V.emit(lambda e, xc=xc, bk=bk: e.tensor_copy(xc, pbank[bk][:, :T]),
                       ins=[bf(("pb", bk))], outs=[bf(("xfc", k))])
                V.emit(lambda e, xc=xc: e.tensor_tensor(sqt, xc, xc, OP.mult),
                       ins=[bf(("xfc", k))], outs=[bf("sqt")])
                mmchain(("pb", 4), [(pbank[4][0:1, :T], ones_col[:, :], xc,
                                     k == 0, k == KD - 1)],
                        [bf("ones_col"), bf(("xfc", k))])
                mmchain(("pb", 5), [(pbank[5][0:1, :T], ones_col[:, :], sqt,
                                     k == 0, k == KD - 1)],
                        [bf("ones_col"), bf("sqt")])
            store(x_fm_d.ap()[:, :, tc * T:(tc + 1) * T].rearrange("c p t -> p c t"),
                  xfm_st, ins=[bf(("xfc", k)) for k in range(KD)],
                  outs=[bf(("xfm", tc))])
            S.emit(lambda e: e.activation(row_st[:, :], pbank[4][0:1, :T], AF.Copy),
                   ins=[bf(("pb", 4))], outs=[bf("row_st")])
            S.emit(lambda e: e.activation(row_st2[:, :], pbank[5][0:1, :T], AF.Copy),
                   ins=[bf(("pb", 5))], outs=[bf("row_st2")])
            store(rows_d.ap()[0:1, tc * T:(tc + 1) * T], row_st[:, :],
                  ins=[bf("row_st")], outs=[bf(("rowst", 0, tc))])
            store(rows_d.ap()[1:2, tc * T:(tc + 1) * T], row_st2[:, :],
                  ins=[bf("row_st2")], outs=[bf(("rowst", 1, tc))])

    def stats_finalize(src0, dst_rs, in_bufs):
        m_ = s64a[:, 0:LP]
        q_ = s64a[:, LP:2 * LP]
        v_ = s64a[:, 2 * LP:3 * LP]
        t_ = s64a[:, 3 * LP:4 * LP]
        load(m_, rows_d.ap()[src0:src0 + 1, :].rearrange("o (p j) -> p o j", p=128),
             outs=[bf("s64m")], ins=in_bufs)
        load(q_, rows_d.ap()[src0 + 1:src0 + 2, :]
             .rearrange("o (p j) -> p o j", p=128),
             outs=[bf("s64q")], ins=in_bufs)
        V.emit(lambda e: e.tensor_scalar_mul(m_, m_, 1.0 / D), ins=[bf("s64m")],
               outs=[bf("s64m")])
        V.emit(lambda e: e.tensor_scalar_mul(q_, q_, 1.0 / D), ins=[bf("s64q")],
               outs=[bf("s64q")])
        V.emit(lambda e: e.tensor_tensor(t_, m_, m_, OP.mult), ins=[bf("s64m")],
               outs=[bf("s64t")])
        V.emit(lambda e: e.tensor_tensor(v_, q_, t_, OP.subtract),
               ins=[bf("s64q"), bf("s64t")], outs=[bf("s64v")])
        S.emit(lambda e: e.activation(v_, v_, AF.Ln, bias=EPS), ins=[bf("s64v")],
               outs=[bf("s64v")])
        S.emit(lambda e: e.activation(v_, v_, AF.Exp, scale=-0.5), ins=[bf("s64v")],
               outs=[bf("s64v")])
        V.emit(lambda e: e.tensor_tensor(t_, m_, v_, OP.mult),
               ins=[bf("s64m"), bf("s64v")], outs=[bf("s64t")])
        store(rows_d.ap()[dst_rs:dst_rs + 1, :]
              .rearrange("o (p j) -> p o j", p=128), v_,
              ins=[bf("s64v")], outs=[bf(("rowst", dst_rs))])
        store(rows_d.ap()[dst_rs + 1:dst_rs + 2, :]
              .rearrange("o (p j) -> p o j", p=128), t_,
              ins=[bf("s64t")], outs=[bf(("rowst", dst_rs + 1))])

    p_phase()
    stats_finalize(0, 2, [bf(("rowst", r, tc)) for r in (0, 1) for tc in range(NT)])
    barrier()
    if upto == "P":
        es.close()
        return nc

    def bcast_rows(row_rs, tc, bank_rs, bank_mrs):
        ib = [bf(("rowst", row_rs)), bf(("rowst", row_rs + 1))]
        load(stat_st[0:2, :], rows_d.ap()[row_rs:row_rs + 2, tc * T:(tc + 1) * T],
             outs=[bf("stat_st")], ins=ib)
        mmchain(("pb", bank_rs), [(pbank[bank_rs][:, :T], sel4[:, 0:128],
                                   stat_st[:, :], True, True)],
                [bf("sel4"), bf("stat_st")])
        mmchain(("pb", bank_mrs), [(pbank[bank_mrs][:, :T], sel4[:, 128:256],
                                    stat_st[:, :], True, True)],
                [bf("sel4"), bf("stat_st")])

    def ln_apply(x_ap, x_b, bank_rs, bank_mrs, t1_ap, t1_b, out_ap, out_b,
                 sc_col, bi_col, modb):
        V.emit(lambda e: e.scalar_tensor_tensor(
            t1_ap, x_ap, 1.0, pbank[bank_rs][:, :T], OP.bypass, OP.mult),
            ins=[x_b, bf(("pb", bank_rs))], outs=[t1_b])
        V.emit(lambda e: e.tensor_tensor(t1_ap, t1_ap, pbank[bank_mrs][:, :T],
                                         OP.subtract),
               ins=[t1_b, bf(("pb", bank_mrs))], outs=[t1_b])
        V.emit(lambda e: e.tensor_scalar(out_ap, t1_ap, sc_col, bi_col,
                                         OP.mult, OP.add),
               ins=[t1_b, modb], outs=[out_b])

    def ln_wide(x_ap, x_b, out_ap, out_b, bias_v, scale_v, mtab, mtab_b):
        rsb = pbank[6][:, :T].unsqueeze(1).broadcast_to([128, KD, T])
        mrsb = pbank[7][:, :T].unsqueeze(1).broadcast_to([128, KD, T])
        x3 = x_ap.rearrange("p (k t) -> p k t", k=KD)
        o3 = out_ap.rearrange("p (k t) -> p k t", k=KD)
        V.emit(lambda e: e.tensor_tensor(o3, x3, rsb, OP.mult),
               ins=[x_b, bf(("pb", 6))], outs=[out_b])
        V.emit(lambda e: e.tensor_tensor(o3, o3, mrsb, OP.subtract),
               ins=[out_b, bf(("pb", 7))], outs=[out_b])
        scv = mtab[:, scale_v:scale_v + KD].unsqueeze(2).broadcast_to([128, KD, T])
        biv = mtab[:, bias_v:bias_v + KD].unsqueeze(2).broadcast_to([128, KD, T])
        V.emit(lambda e: e.tensor_tensor(o3, o3, scv, OP.mult),
               ins=[out_b, mtab_b], outs=[out_b])
        V.emit(lambda e: e.tensor_tensor(o3, o3, biv, OP.add),
               ins=[out_b, mtab_b], outs=[out_b])

    def load_w_pair(w_dram, col0, ncol, hi_off, lo_off, tag, wstg, wstg_key):
        nk = w_dram.shape[0] // 128
        for k in range(nk):
            load(wstg[:, :ncol], w_dram.ap()[k * 128:(k + 1) * 128, col0:col0 + ncol],
                 outs=[bf(wstg_key)])
            split_pair(wstg[:, :ncol], bsl(hi_off + k * ncol, ncol),
                       bsl(lo_off + k * ncol, ncol),
                       bf(wstg_key), bf((tag, "h", k)), bf((tag, "l", k)))

    wA_h, wA_l = 0, KD * 1024
    hA_h, hA_l = 2 * KD * 1024, 2 * KD * 1024 + KD * T

    def a1_phase():
        xfmt = [fsl(0, KD * T), fsl(KD * T, KD * T)]
        t1t = [fsl(2 * KD * T, T), fsl(2 * KD * T + T, T)]
        CV = T + 3
        h1t = [fsl(2 * KD * T + 2 * T + 2 * CV + 12 * T, T),
               fsl(2 * KD * T + 2 * T + 2 * CV + 13 * T, T)]
        xcv = [fsl(2 * KD * T + 2 * T, CV), fsl(2 * KD * T + 2 * T + CV, CV)]
        ut = fsl(2 * KD * T + 2 * T + 2 * CV, 8 * T)
        sgt = [fsl(2 * KD * T + 2 * T + 2 * CV + 8 * T, T),
               fsl(2 * KD * T + 2 * T + 2 * CV + 9 * T, T)]
        wstg = fsl(2 * KD * T + 2 * T + 2 * CV + 10 * T, 1024)
        for og in range(2):
            load_w_pair(w_in_T, og * 1024, 1024, wA_h, wA_l, "wA", wstg, "wstgA")
            for tc in range(NT):
                sl = tc % 2
                if og == 0:
                    load(xfmt[sl], x_fm_d.ap()[:, :, tc * T:(tc + 1) * T]
                         .rearrange("c p t -> p c t"),
                         outs=[bf(("xfmt", sl))], ins=[bf(("xfm", tc))])
                    bcast_rows(2, tc, 6, 7)
                    for k in range(KD):
                        q = k % 2
                        ln_apply(xfmt[sl][:, k * T:(k + 1) * T], bf(("xfmt", sl)),
                                 6, 7, t1t[q], bf(("t1t", q)), h1t[q],
                                 bf(("h1t", q)),
                                 modc[:, KD + k:KD + k + 1],
                                 modc[:, k:k + 1], bf("modc"))
                        split_pair(h1t[q], bsl(hA_h + k * T, T),
                                   bsl(hA_l + k * T, T),
                                   bf(("h1t", q)), bf(("hh", k)), bf(("hl", k)))
                    store(h_hi_d.ap()[:, :, tc * T:(tc + 1) * T]
                          .rearrange("c p t -> p c t"), bsl(hA_h, KD * T),
                          ins=[bf(("hh", k)) for k in range(KD)],
                          outs=[bf(("hhd", tc))])
                    store(h_lo_d.ap()[:, :, tc * T:(tc + 1) * T]
                          .rearrange("c p t -> p c t"), bsl(hA_l, KD * T),
                          ins=[bf(("hl", k)) for k in range(KD)],
                          outs=[bf(("hld", tc))])
                else:
                    load(bsl(hA_h, KD * T), h_hi_d.ap()[:, :, tc * T:(tc + 1) * T]
                         .rearrange("c p t -> p c t"),
                         outs=[bf(("hh", k)) for k in range(KD)],
                         ins=[bf(("hhd", tc))])
                    load(bsl(hA_l, KD * T), h_lo_d.ap()[:, :, tc * T:(tc + 1) * T]
                         .rearrange("c p t -> p c t"),
                         outs=[bf(("hl", k)) for k in range(KD)],
                         ins=[bf(("hld", tc))])
                for c in range(8):
                    cc = og * 8 + c
                    bk = c % 2
                    mms = []
                    dep = []
                    for k in range(KD):
                        lh = bsl(wA_h + k * 1024 + c * 128, 128)
                        ll = bsl(wA_l + k * 1024 + c * 128, 128)
                        rh = bsl(hA_h + k * T, T)
                        rl = bsl(hA_l + k * T, T)
                        mms += [(pbank[bk][:, :T], lh, rh, k == 0, False),
                                (pbank[bk][:, :T], lh, rl, False, False),
                                (pbank[bk][:, :T], ll, rh, False, k == KD - 1)]
                        dep += [bf(("wA", "h", k)), bf(("wA", "l", k)),
                                bf(("hh", k)), bf(("hl", k))]
                    mmchain(("pb", bk), mms, dep)
                    xv = xcv[c % 2]
                    V.emit(lambda e, xv=xv, cc=cc: e.tensor_copy(
                        xv[:, 0:3], halo[:, cc * 4:cc * 4 + 3]),
                        ins=[bf(("halo", cc))], outs=[bf(("xcv", c % 2))])
                    V.emit(lambda e, xv=xv, bk=bk: e.tensor_copy(
                        xv[:, 3:3 + T], pbank[bk][:, :T]),
                        ins=[bf(("pb", bk))], outs=[bf(("xcv", c % 2))])
                    V.emit(lambda e, xv=xv, cc=cc: e.tensor_copy(
                        halo[:, cc * 4:cc * 4 + 3], xv[:, T:T + 3]),
                        ins=[bf(("xcv", c % 2))], outs=[bf(("halo", cc))])
                    us = ut[:, c * T:(c + 1) * T]
                    sg_ = sgt[c % 2]
                    V.emit(lambda e, xv=xv, us=us, cc=cc: e.tensor_scalar(
                        us, xv[:, 0:T], conv_w[:, cc * 4:cc * 4 + 1], None, OP.mult),
                        ins=[bf(("xcv", c % 2)), bf("conv_w")], outs=[bf(("ut", c))])
                    for w in range(1, 4):
                        V.emit(lambda e, xv=xv, us=us, cc=cc, w=w:
                               e.scalar_tensor_tensor(
                                   us, xv[:, w:w + T],
                                   conv_w[:, cc * 4 + w:cc * 4 + w + 1],
                                   us, OP.mult, OP.add),
                               ins=[bf(("xcv", c % 2)), bf("conv_w"), bf(("ut", c))],
                               outs=[bf(("ut", c))])
                    V.emit(lambda e, us=us, cc=cc: e.tensor_scalar(
                        us, us, conv_b[:, cc:cc + 1], None, OP.add),
                        ins=[bf(("ut", c)), bf("conv_b")], outs=[bf(("ut", c))])
                    S.emit(lambda e, us=us, sg_=sg_: e.activation(
                        sg_, us, AF.Sigmoid),
                        ins=[bf(("ut", c))], outs=[bf(("sgt", c % 2))])
                    G.emit(lambda e, us=us, sg_=sg_: e.tensor_tensor(
                        us, us, sg_, OP.mult),
                        ins=[bf(("ut", c)), bf(("sgt", c % 2))], outs=[bf(("ut", c))])
                store(u_d.ap()[og * 8:(og + 1) * 8, :, tc * T:(tc + 1) * T]
                      .rearrange("c p t -> p c t"), ut,
                      ins=[bf(("ut", c)) for c in range(8)],
                      outs=[bf(("ud", og * 8 + c, tc)) for c in range(8)])
            barrier()

    a1_phase()
    if upto == "A1":
        es.close()
        return nc

    def a2_phase():
        zstg = fsl(0, 8 * T)
        ustg = fsl(8 * T, 8 * T)
        xdst = fsl(16 * T, T, p=XD)
        wstg = fsl(17 * T, 1024)
        uph_o = 2 * KD * 1024 + 2 * KD * T
        upl_o = uph_o + 8 * T
        for og in range(2):
            load_w_pair(w_in_T, DI + og * 1024, 1024, wA_h, wA_l, "wZ", wstg, "wstgZ")
            for tc in range(NT):
                load(bsl(hA_h, KD * T), h_hi_d.ap()[:, :, tc * T:(tc + 1) * T]
                     .rearrange("c p t -> p c t"),
                     outs=[bf(("hh", k)) for k in range(KD)], ins=[bf(("hhd", tc))])
                load(bsl(hA_l, KD * T), h_lo_d.ap()[:, :, tc * T:(tc + 1) * T]
                     .rearrange("c p t -> p c t"),
                     outs=[bf(("hl", k)) for k in range(KD)], ins=[bf(("hld", tc))])
                for c in range(8):
                    cc = og * 8 + c
                    bk = c % 2
                    mms = []
                    dep = []
                    for k in range(KD):
                        lh = bsl(wA_h + k * 1024 + c * 128, 128)
                        ll = bsl(wA_l + k * 1024 + c * 128, 128)
                        rh = bsl(hA_h + k * T, T)
                        rl = bsl(hA_l + k * T, T)
                        mms += [(pbank[bk][:, :T], lh, rh, k == 0, False),
                                (pbank[bk][:, :T], lh, rl, False, False),
                                (pbank[bk][:, :T], ll, rh, False, k == KD - 1)]
                        dep += [bf(("wZ", "h", k)), bf(("wZ", "l", k)),
                                bf(("hh", k)), bf(("hl", k))]
                    mmchain(("pb", bk), mms, dep)
                    V.emit(lambda e, c=c, bk=bk: e.tensor_copy(
                        zstg[:, c * T:(c + 1) * T], pbank[bk][:, :T]),
                        ins=[bf(("pb", bk))], outs=[bf(("zst", c))])
                store(z_d.ap()[og * 8:(og + 1) * 8, :, tc * T:(tc + 1) * T]
                      .rearrange("c p t -> p c t"), zstg,
                      ins=[bf(("zst", c)) for c in range(8)],
                      outs=[bf(("zd", og * 8 + c, tc)) for c in range(8)])
            barrier()
        for tc in range(NT):
            for half in range(2):
                load(ustg, u_d.ap()[half * 8:(half + 1) * 8, :, tc * T:(tc + 1) * T]
                     .rearrange("c p t -> p c t"),
                     outs=[bf("ustg")],
                     ins=[bf(("ud", half * 8 + c, tc)) for c in range(8)])
                for k in range(8):
                    split_pair(ustg[:, k * T:(k + 1) * T], bsl(uph_o + k * T, T),
                               bsl(upl_o + k * T, T), bf("ustg"),
                               bf(("uph", k)), bf(("upl", k)))
                mms = []
                dep = [bf("xpj_h"), bf("xpj_l")]
                for k in range(8):
                    kk = half * 8 + k
                    lh = xpj_h[:, kk * XD:(kk + 1) * XD]
                    ll = xpj_l[:, kk * XD:(kk + 1) * XD]
                    rh = bsl(uph_o + k * T, T)
                    rl = bsl(upl_o + k * T, T)
                    mms += [(pbank[4][:XD, :T], lh, rh, half == 0 and k == 0, False),
                            (pbank[4][:XD, :T], lh, rl, False, False),
                            (pbank[4][:XD, :T], ll, rh, False,
                             half == 1 and k == 7)]
                    dep += [bf(("uph", k)), bf(("upl", k))]
                mmchain(("pb", 4), mms, dep)
            V.emit(lambda e: e.tensor_copy(xdst, pbank[4][:XD, :T]),
                   ins=[bf(("pb", 4))], outs=[bf("xdst")])
            store(xdbl_d.ap()[:, tc * T:(tc + 1) * T], xdst, ins=[bf("xdst")],
                  outs=[bf(("xdbl", tc))])
        barrier()

    a2_phase()
    if upto == "A2":
        es.close()
        return nc

    def sc_phase():
        NB = 4
        WB = NB * T
        sel32 = bsl(2 * T, 32 * 128, p=32)
        load(sel32, sel32_in.ap(), outs=[bf("sel32")])
        dstp = fsl(0, WB)
        wstp = fsl(WB, WB)
        ystp = fsl(2 * WB, WB)
        estp = fsl(2 * WB, WB)
        ustp = fsl(3 * WB, WB)
        dAf = fsl(4 * WB, WB)
        bbf = [fsl(5 * WB, WB), fsl(6 * WB, WB)]
        hff = [fsl(7 * WB, WB), fsl(8 * WB, WB)]
        hlast = fsl(9 * WB, KDI * NS)
        cev = [fsl(9 * WB + KDI * NS + T, T), fsl(9 * WB + KDI * NS + 2 * T, T)]
        dtl_f = fsl(9 * WB + KDI * NS, T, p=RK)
        dtl_h = bsl(0, T, p=RK)
        dtl_l = bsl(T, T, p=RK)
        hl_tok = V.emit(lambda e: e.memset(hlast, 0.0), outs=[bf("hlast")])
        for _c in range(KDI):
            for _n in range(NS):
                bf(("hl8", _c, _n)).w = hl_tok
        for tc in range(NT):
            load(dtl_f, xdbl_d.ap()[0:RK, tc * T:(tc + 1) * T],
                 outs=[bf("dtl_f")], ins=[bf(("xdbl", tc))])
            split_pair(dtl_f, dtl_h, dtl_l, bf("dtl_f"), bf("dtl_h"), bf("dtl_l"))
            load(bc_f[:, :], xdbl_d.ap()[RK:RK + 32, tc * T:(tc + 1) * T],
                 outs=[bf("bc_f")], ins=[bf(("xdbl", tc))])
            split_pair(bc_f[:, :], bc_h[:, :], bc_l[:, :], bf("bc_f"),
                       bf("bc_h"), bf("bc_l"))
            for cb in range(KDI // NB):
                c0 = cb * NB
                for c4 in range(NB):
                    c = c0 + c4
                    bk = c4 % 2
                    dck = dstp[:, c4 * T:(c4 + 1) * T]
                    mmchain(("pb", bk),
                            [(pbank[bk][:, :T], dtw_h[:, c * 128:(c + 1) * 128],
                              dtl_h, True, False),
                             (pbank[bk][:, :T], dtw_h[:, c * 128:(c + 1) * 128],
                              dtl_l, False, False),
                             (pbank[bk][:, :T], dtw_l[:, c * 128:(c + 1) * 128],
                              dtl_h, False, True)],
                            [bf("dtw_h"), bf("dtw_l"), bf("dtl_h"), bf("dtl_l")])
                    V.emit(lambda e, dck=dck, bk=bk, c=c: e.tensor_scalar(
                        dck, pbank[bk][:, :T], dt_b[:, c:c + 1], None, OP.add),
                        ins=[bf(("pb", bk)), bf("dt_b")], outs=[bf("dstp")])
                S.emit(lambda e: e.activation(estp, dstp, AF.Abs),
                       ins=[bf("dstp")], outs=[bf("estp")])
                S.emit(lambda e: e.activation(estp, estp, AF.Exp, scale=-1.0),
                       ins=[bf("estp")], outs=[bf("estp")])
                S.emit(lambda e: e.activation(estp, estp, AF.Ln, bias=1.0),
                       ins=[bf("estp")], outs=[bf("estp")])
                V.emit(lambda e: e.scalar_tensor_tensor(
                    dstp, dstp, 0.0, estp, OP.max, OP.add),
                    ins=[bf("dstp"), bf("estp")], outs=[bf("dstp")])
                load(ustp, u_d.ap()[c0:c0 + NB, :, tc * T:(tc + 1) * T]
                     .rearrange("c p t -> p c t"),
                     outs=[bf("ustp")], ins=[bf(("ud", c0 + i, tc)) for i in range(NB)])
                G.emit(lambda e: e.tensor_tensor(wstp, dstp, ustp, OP.mult),
                       ins=[bf("dstp"), bf("ustp")], outs=[bf("wstp")])
                dAb = [bf("dAf"), bf("ustp")]
                for n in range(NS):
                    n2 = n % 2
                    bkB = 2 + n2
                    bkC = 4 + n2
                    mmchain(("pb", bkB),
                            [(pbank[bkB][:, :T], sel32[:, n * 128:(n + 1) * 128],
                              bc_h[:, :], True, False),
                             (pbank[bkB][:, :T], sel32[:, n * 128:(n + 1) * 128],
                              bc_l[:, :], False, True)],
                            [bf("sel32"), bf("bc_h"), bf("bc_l")])
                    mmchain(("pb", bkC),
                            [(pbank[bkC][:, :T],
                              sel32[:, (16 + n) * 128:(17 + n) * 128],
                              bc_h[:, :], True, False),
                             (pbank[bkC][:, :T],
                              sel32[:, (16 + n) * 128:(17 + n) * 128],
                              bc_l[:, :], False, True)],
                            [bf("sel32"), bf("bc_h"), bf("bc_l")])
                    bbv = pbank[bkB][:, :T].unsqueeze(1).broadcast_to([128, NB, T])
                    ccv = pbank[bkC][:, :T].unsqueeze(1).broadcast_to([128, NB, T])
                    bb_ = bbf[n2]
                    hf_ = hff[n2]
                    dA_ = dAf if n2 == 0 else ustp
                    for c4 in range(NB):
                        c = c0 + c4
                        S.emit(lambda e, dA_=dA_, c4=c4, c=c, n=n: e.activation(
                            dA_[:, c4 * T:(c4 + 1) * T],
                            dstp[:, c4 * T:(c4 + 1) * T], AF.Exp,
                            scale=a_neg[:, c * NS + n:c * NS + n + 1]),
                            ins=[bf("dstp"), bf("a_neg")], outs=[dAb[n2]])
                    V.emit(lambda e, bb_=bb_, bbv=bbv: e.tensor_tensor(
                        bb_.rearrange("p (c t) -> p c t", c=NB),
                        wstp.rearrange("p (c t) -> p c t", c=NB), bbv, OP.mult),
                        ins=[bf("wstp"), bf(("pb", bkB))], outs=[bf(("bbf", n2))])
                    for c4 in range(NB):
                        c = c0 + c4
                        V.emit(lambda e, hf_=hf_, bb_=bb_, dA_=dA_, c4=c4, c=c, n=n:
                               e.tensor_tensor_scan(
                                   hf_[:, c4 * T:(c4 + 1) * T],
                                   dA_[:, c4 * T:(c4 + 1) * T],
                                   bb_[:, c4 * T:(c4 + 1) * T],
                                   hlast[:, c * NS + n:c * NS + n + 1],
                                   OP.mult, OP.add),
                               ins=[dAb[n2], bf(("bbf", n2)), bf(("hl8", c, n))],
                               outs=[bf(("hff", n2))])
                        S.emit(lambda e, hf_=hf_, c4=c4, c=c, n=n: e.activation(
                            hlast[:, c * NS + n:c * NS + n + 1],
                            hf_[:, c4 * T + T - 1:c4 * T + T], AF.Copy),
                            ins=[bf(("hff", n2))], outs=[bf(("hl8", c, n))])
                    V.emit(lambda e, hf_=hf_, bb_=bb_, ccv=ccv: e.tensor_tensor(
                        bb_.rearrange("p (c t) -> p c t", c=NB),
                        hf_.rearrange("p (c t) -> p c t", c=NB), ccv, OP.mult),
                        ins=[bf(("hff", n2)), bf(("pb", bkC))], outs=[bf(("bbf", n2))])
                    if n == 0:
                        G.emit(lambda e, bb_=bb_: e.tensor_copy(ystp, bb_),
                               ins=[bf(("bbf", n2))], outs=[bf("estp")])
                    else:
                        G.emit(lambda e, bb_=bb_: e.tensor_tensor(
                            ystp, ystp, bb_, OP.add),
                            ins=[bf(("bbf", n2)), bf("estp")], outs=[bf("estp")])
                load(dAf, u_d.ap()[c0:c0 + NB, :, tc * T:(tc + 1) * T]
                     .rearrange("c p t -> p c t"),
                     outs=[bf("dAf")], ins=[bf(("ud", c0 + i, tc)) for i in range(NB)])
                for c4 in range(NB):
                    c = c0 + c4
                    yck = ystp[:, c4 * T:(c4 + 1) * T]
                    V.emit(lambda e, yck=yck, c=c, c4=c4: e.scalar_tensor_tensor(
                        yck, dAf[:, c4 * T:(c4 + 1) * T], d_par[:, c:c + 1],
                        yck, OP.mult, OP.add),
                        ins=[bf("dAf"), bf("d_par"), bf("estp")], outs=[bf("estp")])
                store(y_d.ap()[c0:c0 + NB, :, tc * T:(tc + 1) * T]
                      .rearrange("c p t -> p c t"), ystp,
                      ins=[bf("estp")],
                      outs=[bf(("yd", c0 + i, tc)) for i in range(NB)])
        barrier()

    sc_phase()
    if upto == "SC":
        es.close()
        return nc

    def d_gating():
        W4 = 4 * T
        yzt = [fsl(0, W4), fsl(W4, W4)]
        zt = [fsl(2 * W4, W4), fsl(3 * W4, W4)]
        sgt2 = [fsl(4 * W4, W4), fsl(5 * W4, W4)]
        ygh = [bsl(0, W4), bsl(W4, W4)]
        ygl = [bsl(2 * W4, W4), bsl(3 * W4, W4)]
        for tc in range(NT):
            for cg in range(KDI // 4):
                p2 = cg % 2
                load(yzt[p2], y_d.ap()[cg * 4:(cg + 1) * 4, :,
                                       tc * T:(tc + 1) * T]
                     .rearrange("c p t -> p c t"),
                     outs=[bf(("yzt", p2))],
                     ins=[bf(("yd", cg * 4 + i, tc)) for i in range(4)])
                load(zt[p2], z_d.ap()[cg * 4:(cg + 1) * 4, :,
                                      tc * T:(tc + 1) * T]
                     .rearrange("c p t -> p c t"),
                     outs=[bf(("zt", p2))],
                     ins=[bf(("zd", cg * 4 + i, tc)) for i in range(4)])
                S.emit(lambda e, p2=p2: e.activation(sgt2[p2], zt[p2],
                                                     AF.Sigmoid),
                       ins=[bf(("zt", p2))], outs=[bf(("sgt2", p2))])
                V.emit(lambda e, p2=p2: e.tensor_tensor(zt[p2], zt[p2], sgt2[p2],
                                                        OP.mult),
                       ins=[bf(("zt", p2)), bf(("sgt2", p2))], outs=[bf(("zt", p2))])
                V.emit(lambda e, p2=p2: e.tensor_tensor(yzt[p2], yzt[p2], zt[p2],
                                                        OP.mult),
                       ins=[bf(("yzt", p2)), bf(("zt", p2))], outs=[bf(("yzt", p2))])
                split_pair(yzt[p2], ygh[p2], ygl[p2], bf(("yzt", p2)),
                           bf(("yghb", p2)), bf(("yglb", p2)), eng=G)
                store(ygh_d.ap()[cg * 4:(cg + 1) * 4, :, tc * T:(tc + 1) * T]
                      .rearrange("c p t -> p c t"), ygh[p2],
                      ins=[bf(("yghb", p2))],
                      outs=[bf(("ygh", cg * 4 + i, tc)) for i in range(4)])
                store(ygl_d.ap()[cg * 4:(cg + 1) * 4, :, tc * T:(tc + 1) * T]
                      .rearrange("c p t -> p c t"), ygl[p2],
                      ins=[bf(("yglb", p2))],
                      outs=[bf(("ygl", cg * 4 + i, tc)) for i in range(4)])
        barrier()

    def d_proj():
        wO_h, wO_l = 0, KDI * 512
        ygh_o = 2 * KDI * 512
        ygl_o = ygh_o + KDI * T
        xft = fsl(0, 4 * T)
        x2t = fsl(4 * T, 4 * T)
        wstg = fsl(8 * T, 512)
        for og in range(2):
            load_w_pair(w_out_T, og * 512, 512, wO_h, wO_l, "wO", wstg, "wstgO")
            for tc in range(NT):
                load(bsl(ygh_o, KDI * T), ygh_d.ap()[:, :, tc * T:(tc + 1) * T]
                     .rearrange("c p t -> p c t"),
                     outs=[bf(("y2h", k)) for k in range(KDI)],
                     ins=[bf(("ygh", k, tc)) for k in range(KDI)])
                load(bsl(ygl_o, KDI * T), ygl_d.ap()[:, :, tc * T:(tc + 1) * T]
                     .rearrange("c p t -> p c t"),
                     outs=[bf(("y2l", k)) for k in range(KDI)],
                     ins=[bf(("ygl", k, tc)) for k in range(KDI)])
                load(xft, x_fm_d.ap()[og * 4:(og + 1) * 4, :, tc * T:(tc + 1) * T]
                     .rearrange("c p t -> p c t"),
                     outs=[bf("xft")], ins=[bf(("xfm", tc))])
                for o4 in range(4):
                    o = og * 4 + o4
                    bk = o4 % 2
                    mms = []
                    dep = []
                    for k in range(KDI):
                        lh = bsl(wO_h + k * 512 + o4 * 128, 128)
                        ll = bsl(wO_l + k * 512 + o4 * 128, 128)
                        rh = bsl(ygh_o + k * T, T)
                        rl = bsl(ygl_o + k * T, T)
                        mms += [(pbank[bk][:, :T], lh, rh, k == 0, False),
                                (pbank[bk][:, :T], lh, rl, False, False),
                                (pbank[bk][:, :T], ll, rh, False, k == KDI - 1)]
                        dep += [bf(("wO", "h", k)), bf(("wO", "l", k)),
                                bf(("y2h", k)), bf(("y2l", k))]
                    mmchain(("pb", bk), mms, dep)
                    x2c = x2t[:, o4 * T:(o4 + 1) * T]
                    V.emit(lambda e, x2c=x2c, o=o, o4=o4, bk=bk:
                           e.scalar_tensor_tensor(
                               x2c, pbank[bk][:, :T],
                               modc[:, 2 * KD + o:2 * KD + o + 1],
                               xft[:, o4 * T:(o4 + 1) * T], OP.mult, OP.add),
                           ins=[bf(("pb", bk)), bf("modc"), bf("xft")],
                           outs=[bf(("x2c", o4))])
                store(x2_d.ap()[og * 4:(og + 1) * 4, :, tc * T:(tc + 1) * T]
                      .rearrange("c p t -> p c t"), x2t,
                      ins=[bf(("x2c", o4)) for o4 in range(4)],
                      outs=[bf(("x2d", og, tc))])
            barrier()

    d_gating()
    d_proj()
    if upto == "D":
        es.close()
        return nc

    def cross_max_to_cc(col_idx, key, slot):
        store(colb_d.ap(), mcol[:, col_idx:col_idx + 1], ins=[bf("mcol")],
              outs=[bf("colb")])
        load(r128[:, :], colb_d.ap().rearrange("p o -> o p"),
             outs=[bf("r128")], ins=[bf("colb")])
        V.emit(lambda e: e.tensor_reduce(r1t[0:1, 0:1], r128[:, :], AX, OP.max),
               ins=[bf("r128")], outs=[bf("r1t")])
        V.emit(lambda e: e.tensor_scalar(r1t[0:1, 0:1], r1t[0:1, 0:1], 1.0 / QM,
                                         1e-12, OP.mult, OP.max),
               ins=[bf("r1t")], outs=[bf("r1t")])
        store(cc_i[key].ap()[0:1, slot:slot + 1], r1t[0:1, 0:1], ins=[bf("r1t")],
              outs=[bf(("cci", key, slot))])

    def cc_run(key, nslots):
        G.emit(lambda e: e.collective_compute(
            "AllReduce", OP.max, replica_groups=[core_ids],
            ins=[cc_i[key].ap()], outs=[cc_o[key].ap()]),
            ins=[bf(("cci", key, s)) for s in range(nslots)],
            outs=[bf(("cco", key))])

    def scalars_to_cols(key, pairs):
        w = cc_o[key].shape[1]
        load(row_st[0:1, 0:w], cc_o[key].ap(), outs=[bf("srow")],
             ins=[bf(("cco", key))])
        for (si, dcol, rcol) in pairs:
            mmchain(("pb", 7), [(pbank[7][:, 0:1], ones_row[:, :],
                                 row_st[0:1, si:si + 1], True, True)],
                    [bf("ones_row"), bf("srow")])
            V.emit(lambda e, dcol=dcol: e.tensor_copy(rcols[:, dcol:dcol + 1],
                                                      pbank[7][:, 0:1]),
                   ins=[bf(("pb", 7))], outs=[bf("rcols")])
            if rcol is not None:
                V.emit(lambda e, dcol=dcol, rcol=rcol: e.reciprocal(
                    rcols[:, rcol:rcol + 1], rcols[:, dcol:dcol + 1]),
                    ins=[bf("rcols")], outs=[bf("rcols")])

    def e0_stats():
        x2f = [fsl(0, KD * T), fsl(KD * T, KD * T)]
        sqt = fsl(2 * KD * T, KD * T)
        for tc in range(NT):
            sl = tc % 2
            load(x2f[sl], x2_d.ap()[:, :, tc * T:(tc + 1) * T]
                 .rearrange("c p t -> p c t"),
                 outs=[bf(("x2f", sl))],
                 ins=[bf(("x2d", 0, tc)), bf(("x2d", 1, tc))])
            V.emit(lambda e, sl=sl: e.tensor_tensor(sqt, x2f[sl], x2f[sl], OP.mult),
                   ins=[bf(("x2f", sl))], outs=[bf("sqt2")])
            mmchain(("pb", 4), [(pbank[4][0:1, :T], ones_col[:, :],
                                 x2f[sl][:, k * T:(k + 1) * T],
                                 k == 0, k == KD - 1) for k in range(KD)],
                    [bf("ones_col"), bf(("x2f", sl))])
            mmchain(("pb", 5), [(pbank[5][0:1, :T], ones_col[:, :],
                                 sqt[:, k * T:(k + 1) * T],
                                 k == 0, k == KD - 1) for k in range(KD)],
                    [bf("ones_col"), bf("sqt2")])
            S.emit(lambda e: e.activation(row_st[:, :], pbank[4][0:1, :T], AF.Copy),
                   ins=[bf(("pb", 4))], outs=[bf("row_st")])
            S.emit(lambda e: e.activation(row_st2[:, :], pbank[5][0:1, :T], AF.Copy),
                   ins=[bf(("pb", 5))], outs=[bf("row_st2")])
            store(rows_d.ap()[4:5, tc * T:(tc + 1) * T], row_st[:, :],
                  ins=[bf("row_st")], outs=[bf(("rowst", 4, tc))])
            store(rows_d.ap()[5:6, tc * T:(tc + 1) * T], row_st2[:, :],
                  ins=[bf("row_st2")], outs=[bf(("rowst", 5, tc))])
        stats_finalize(4, 6,
                       [bf(("rowst", r, tc)) for r in (4, 5) for tc in range(NT)])
        barrier()

    def e0_max():
        x2f = [fsl(0, KD * T), fsl(KD * T, KD * T)]
        h2 = fsl(2 * KD * T, KD * T)
        V.emit(lambda e: e.memset(mcol[:, 0:1], 0.0), ins=[bf("mcol")],
               outs=[bf("mcol")])
        for tc in range(NT):
            sl = tc % 2
            load(x2f[sl], x2_d.ap()[:, :, tc * T:(tc + 1) * T]
                 .rearrange("c p t -> p c t"),
                 outs=[bf(("x2f", sl))],
                 ins=[bf(("x2d", 0, tc)), bf(("x2d", 1, tc))])
            bcast_rows(6, tc, 6, 7)
            ln_wide(x2f[sl], bf(("x2f", sl)), h2, bf("h2e0"), 3 * KD, 4 * KD,
                    modc, bf("modc"))
            V.emit(lambda e: e.tensor_reduce(mcol[:, 7:8], h2, AX, OP.max,
                                             apply_absolute_value=True),
                   ins=[bf("h2e0")], outs=[bf("mcol")])
            V.emit(lambda e: e.tensor_tensor(mcol[:, 0:1], mcol[:, 0:1],
                                             mcol[:, 7:8], OP.max),
                   ins=[bf("mcol")], outs=[bf("mcol")])
        cross_max_to_cc(0, "1", 0)
        cc_run("1", 1)
        scalars_to_cols("1", [(0, 0, 1)])
        V.emit(lambda e: e.tensor_scalar(
            modc2[:, 0:KD], modc[:, 4 * KD:5 * KD], rcols[:, 1:2], None, OP.mult),
            ins=[bf("modc"), bf("rcols")], outs=[bf("modc2")])
        V.emit(lambda e: e.tensor_scalar(
            modc2[:, KD:2 * KD], modc[:, 3 * KD:4 * KD], rcols[:, 1:2], None,
            OP.mult),
            ins=[bf("modc"), bf("rcols")], outs=[bf("modc2")])
        barrier()

    e0_stats()
    e0_max()
    if upto == "E0":
        es.close()
        return nc

    def e1_quant():
        W8 = KD * T
        x2f = [fsl(0, W8), fsl(W8, W8)]
        h2s = fsl(2 * W8, W8)
        mth = fsl(3 * W8, W8)
        rth = fsl(4 * W8, W8)
        k_h = bsl(0, W8)
        k_l = bsl(W8, W8)
        for tc in range(NT):
            sl = tc % 2
            load(x2f[sl], x2_d.ap()[:, :, tc * T:(tc + 1) * T]
                 .rearrange("c p t -> p c t"),
                 outs=[bf(("x2f", sl))],
                 ins=[bf(("x2d", 0, tc)), bf(("x2d", 1, tc))])
            bcast_rows(6, tc, 6, 7)
            ln_wide(x2f[sl], bf(("x2f", sl)), h2s, bf("h2s"), KD, 0,
                    modc2, bf("modc2"))
            rne(h2s, bf("h2s"), rth, bf("rth"), mth, bf("mth"),
                x2f[sl], bf(("x2f", sl)))
            split256(rth, bf("rth"), k_h, bf("khh"), k_l, bf("kll"),
                     mth, bf("mth"))
            store(k1h_d.ap()[:, :, tc * T:(tc + 1) * T].rearrange("c p t -> p c t"),
                  k_h, ins=[bf("khh")], outs=[bf(("k1h", tc))])
            store(k1l_d.ap()[:, :, tc * T:(tc + 1) * T].rearrange("c p t -> p c t"),
                  k_l, ins=[bf("kll")], outs=[bf(("k1l", tc))])
        barrier()

    def lpbq_group(w_dram, tag, og, nko):
        I = w_dram.shape[1]
        NB = I // 32
        KI = I // 128
        wraw = fsl(0, I)
        wsc = fsl(I, NB)
        wm2 = fsl(I + NB, I)
        wdq = fsl(2 * I + NB, I)
        wtr = fsl(3 * I + NB, 128)
        for oi in range(nko):
            oc = og * nko + oi
            load(wraw, w_dram.ap()[oc * 128:(oc + 1) * 128, :], outs=[bf("wraw")])
            V.emit(lambda e: e.tensor_reduce(
                wsc, wraw.rearrange("p (b i) -> p b i", i=32), AX, OP.max,
                apply_absolute_value=True),
                ins=[bf("wraw")], outs=[bf("wsc")])
            V.emit(lambda e: e.tensor_scalar(wsc, wsc, 1.0 / 7.0, 1e-12,
                                             OP.mult, OP.max),
                   ins=[bf("wsc")], outs=[bf("wsc")])
            V.emit(lambda e, NB=NB: e.reciprocal(wm2[:, 0:NB], wsc),
                   ins=[bf("wsc")], outs=[bf("wmr")])
            V.emit(lambda e, NB=NB: e.tensor_tensor(
                wdq.rearrange("p (b i) -> p b i", i=32),
                wraw.rearrange("p (b i) -> p b i", i=32),
                wm2[:, 0:NB].unsqueeze(2).broadcast_to([128, NB, 32]), OP.mult),
                ins=[bf("wraw"), bf("wmr")], outs=[bf("wdq")])
            rne(wdq, bf("wdq"), wdq, bf("wdq"), wraw, bf("wraw"), wm2, bf("wm2"))
            V.emit(lambda e, NB=NB: e.tensor_tensor(
                wdq.rearrange("p (b i) -> p b i", i=32),
                wdq.rearrange("p (b i) -> p b i", i=32),
                wsc.unsqueeze(2).broadcast_to([128, NB, 32]), OP.mult),
                ins=[bf("wdq"), bf("wsc")], outs=[bf("wdq")])
            for k in range(KI):
                bk = k % 2
                trchain(("pb", bk), [(pbank[bk][:, :128],
                                      wdq[:, k * 128:(k + 1) * 128])], [bf("wdq")])
                V.emit(lambda e, bk=bk: e.tensor_copy(wtr, pbank[bk][:, :128]),
                       ins=[bf(("pb", bk))], outs=[bf("wtr")])
                split_pair(wtr, bsl(k * (nko * 128) + oi * 128, 128),
                           bsl(KI * nko * 128 + k * (nko * 128) + oi * 128, 128),
                           bf("wtr"), bf((tag, "h", k)), bf((tag, "l", k)))

    def ffn_mm(w_dram, tag, out_dram, out_key, mx_idx):
        kh_o = 2 * KD * 12 * 128
        kl_o = kh_o + KD * T
        ustg = fsl(4 * D + D // 32 + 256, 12 * T)
        V.emit(lambda e, mx_idx=mx_idx: e.memset(mcol[:, mx_idx:mx_idx + 1], 0.0),
               ins=[bf("mcol")], outs=[bf("mcol")])
        for og in range(2):
            lpbq_group(w_dram, tag, og, 12)
            for tc in range(NT):
                load(bsl(kh_o, KD * T), k1h_d.ap()[:, :, tc * T:(tc + 1) * T]
                     .rearrange("c p t -> p c t"),
                     outs=[bf(("k2h", k)) for k in range(KD)], ins=[bf(("k1h", tc))])
                load(bsl(kl_o, KD * T), k1l_d.ap()[:, :, tc * T:(tc + 1) * T]
                     .rearrange("c p t -> p c t"),
                     outs=[bf(("k2l", k)) for k in range(KD)], ins=[bf(("k1l", tc))])
                for oi in range(12):
                    bk = oi % 2
                    mms = []
                    dep = []
                    for k in range(KD):
                        lh = bsl(k * 1536 + oi * 128, 128)
                        ll = bsl(KD * 1536 + k * 1536 + oi * 128, 128)
                        rh = bsl(kh_o + k * T, T)
                        rl = bsl(kl_o + k * T, T)
                        mms += [(pbank[bk][:, :T], lh, rh, k == 0, False),
                                (pbank[bk][:, :T], lh, rl, False, False),
                                (pbank[bk][:, :T], ll, rh, False, k == KD - 1)]
                        dep += [bf((tag, "h", k)), bf((tag, "l", k)),
                                bf(("k2h", k)), bf(("k2l", k))]
                    mmchain(("pb", bk), mms, dep)
                    usl = ustg[:, oi * T:(oi + 1) * T]
                    S.emit(lambda e, usl=usl, bk=bk: e.activation(
                        usl, pbank[bk][:, :T], AF.Identity, bias=0.0,
                        scale=rcols[:, 0:1]),
                        ins=[bf(("pb", bk)), bf("rcols")], outs=[bf(("ust", oi))])
                    V.emit(lambda e, usl=usl: e.tensor_reduce(
                        mcol[:, 7:8], usl, AX, OP.max, apply_absolute_value=True),
                        ins=[bf(("ust", oi))], outs=[bf("mcol")])
                    V.emit(lambda e, mx_idx=mx_idx: e.tensor_tensor(
                        mcol[:, mx_idx:mx_idx + 1], mcol[:, mx_idx:mx_idx + 1],
                        mcol[:, 7:8], OP.max),
                        ins=[bf("mcol")], outs=[bf("mcol")])
                store(out_dram.ap()[og * 12:(og + 1) * 12, :, tc * T:(tc + 1) * T]
                      .rearrange("c p t -> p c t"), ustg,
                      ins=[bf(("ust", oi)) for oi in range(12)],
                      outs=[bf((out_key, og * 12 + oi, tc)) for oi in range(12)])
            barrier()

    e1_quant()
    if upto == "E1q":
        es.close()
        return nc
    ffn_mm(w_up_in, "wU", upraw_d, "upr", 2)
    if upto == "E1":
        es.close()
        return nc
    ffn_mm(w_gate_in, "wG", gateraw_d, "gtr", 3)
    cross_max_to_cc(2, "23", 0)
    cross_max_to_cc(3, "23", 1)
    cc_run("23", 2)
    scalars_to_cols("23", [(0, 2, 3), (1, 4, 5)])
    V.emit(lambda e: e.tensor_scalar_mul(rcols[:, 10:11], rcols[:, 4:5],
                                         1.0 / 65536.0),
           ins=[bf("rcols")], outs=[bf("rcols")])
    barrier()
    if upto == "E2":
        es.close()
        return nc

    def f_phase():
        W4 = 4 * T
        stg = {}
        for p in range(2):
            b0 = p * 5 * W4
            stg[p] = (fsl(b0, W4), fsl(b0 + W4, W4), fsl(b0 + 2 * W4, W4),
                      fsl(b0 + 3 * W4, W4), fsl(b0 + 4 * W4, W4))
        V.emit(lambda e: e.memset(mcol[:, 4:5], 0.0), ins=[bf("mcol")],
               outs=[bf("mcol")])
        V.emit(lambda e: e.memset(mcol[:, 1:2], 0.0), ins=[bf("mcol")],
               outs=[bf("mcol")])
        tailq = []
        for tc in range(NT):
            for sg in range(6):
                it = tc * 6 + sg
                gp = it % 4 == 3
                p = 1 if gp else 0
                E = V
                if gp:
                    for _t in tailq:
                        _t()
                    tailq = []
                acc = 1 if p else 4
                tmp = 6 if p else 7
                gstg, g1, mt, rg, ostg = stg[p]
                load(gstg, gateraw_d.ap()[sg * 4:(sg + 1) * 4, :,
                                          tc * T:(tc + 1) * T]
                     .rearrange("c p t -> p c t"),
                     outs=[bf(("gstg", p))],
                     ins=[bf(("gtr", sg * 4 + i, tc)) for i in range(4)])
                if not gp:
                    E.emit(lambda e, g1=g1, gstg=gstg: e.tensor_scalar(
                        g1, gstg, rcols[:, 5:6], None, OP.mult),
                        ins=[bf(("gstg", p)), bf("rcols")], outs=[bf(("g1f", p))])
                    rne_e(E, g1, bf(("g1f", p)), rg, bf(("rgf", p)),
                          mt, bf(("mtf", p)), gstg, bf(("gstg", p)))
                else:
                    bctt(g1, gstg, rcols[:, 5:6], OP.mult,
                         [bf(("gstg", p)), bf("rcols")], [bf(("g1f", p))], W4)
                    rne_g(g1, bf(("g1f", p)), rg, bf(("rgf", p)),
                          mt, bf(("mtf", p)), gstg, bf(("gstg", p)), W4)
                S.emit(lambda e, g1=g1, rg=rg: e.activation(
                    g1, rg, AF.Sigmoid, scale=rcols[:, 4:5]),
                    ins=[bf(("rgf", p)), bf("rcols")], outs=[bf(("g1f", p))])
                if not gp:
                    E.emit(lambda e, g1=g1: e.tensor_scalar_mul(g1, g1, 65536.0),
                           ins=[bf(("g1f", p))], outs=[bf(("g1f", p))])
                    rne_e(E, g1, bf(("g1f", p)), ostg, bf(("ostgF", p)),
                          mt, bf(("mtf", p)), gstg, bf(("gstg", p)))
                    E.emit(lambda e, ostg=ostg: e.tensor_single_scalar(
                        ostg, ostg, 65535.0, OP.min),
                        ins=[bf(("ostgF", p))], outs=[bf(("ostgF", p))])
                    V.emit(lambda e, ostg=ostg, rg=rg: e.scalar_tensor_tensor(
                        ostg, rg, rcols[:, 10:11], ostg, OP.mult, OP.mult),
                        ins=[bf(("rgf", p)), bf("rcols"), bf(("ostgF", p))],
                        outs=[bf(("ostgF", p))])
                else:
                    bctt(g1, g1, consts[:, 5:6], OP.mult,
                         [bf(("g1f", p)), bf("consts")], [bf(("g1f", p))], W4)
                    rne_g(g1, bf(("g1f", p)), ostg, bf(("ostgF", p)),
                          mt, bf(("mtf", p)), gstg, bf(("gstg", p)), W4)
                    bctt(mt, rg, rcols[:, 10:11], OP.mult,
                         [bf(("rgf", p)), bf("rcols")], [bf(("mtf", p))], W4)

                def tail(p=p, gp=gp, ostg=ostg, mt=mt, acc=acc, tmp=tmp,
                         sg=sg, tc=tc):
                    if gp:
                        V.emit(lambda e: e.tensor_single_scalar(
                            ostg, ostg, 65535.0, OP.min),
                            ins=[bf(("ostgF", p))], outs=[bf(("ostgF", p))])
                        V.emit(lambda e: e.tensor_tensor(ostg, mt, ostg,
                                                         OP.mult),
                               ins=[bf(("mtf", p)), bf(("ostgF", p))],
                               outs=[bf(("ostgF", p))])
                    V.emit(lambda e: e.tensor_reduce(
                        mcol[:, tmp:tmp + 1], ostg, AX, OP.max,
                        apply_absolute_value=True),
                        ins=[bf(("ostgF", p))], outs=[bf("mcol")])
                    V.emit(lambda e: e.tensor_tensor(
                        mcol[:, acc:acc + 1], mcol[:, acc:acc + 1],
                        mcol[:, tmp:tmp + 1], OP.max),
                        ins=[bf("mcol")], outs=[bf("mcol")])
                    store(g2raw_d.ap()[sg * 4:(sg + 1) * 4, :,
                                       tc * T:(tc + 1) * T]
                          .rearrange("c p t -> p c t"), ostg,
                          ins=[bf(("ostgF", p))],
                          outs=[bf(("g2r", sg * 4 + i, tc)) for i in range(4)])
                if gp:
                    tailq.append(tail)
                else:
                    tail()
        for _t in tailq:
            _t()
        V.emit(lambda e: e.tensor_tensor(mcol[:, 4:5], mcol[:, 4:5],
                                         mcol[:, 1:2], OP.max),
               ins=[bf("mcol")], outs=[bf("mcol")])
        cross_max_to_cc(4, "4", 0)
        cc_run("4", 1)
        scalars_to_cols("4", [(0, 6, 7)])
        barrier()

    f_phase()
    if upto == "F":
        es.close()
        return nc

    def g_phase():
        W4 = 4 * T
        stg = {}
        for p in range(2):
            b0 = p * 5 * W4
            stg[p] = (fsl(b0, W4), fsl(b0 + W4, W4), fsl(b0 + 2 * W4, W4),
                      fsl(b0 + 3 * W4, W4), fsl(b0 + 4 * W4, W4))
        V.emit(lambda e: e.tensor_tensor(rcols[:, 11:12], rcols[:, 2:3],
                                         rcols[:, 6:7], OP.mult),
               ins=[bf("rcols")], outs=[bf("rcols")])
        V.emit(lambda e: e.memset(mcol[:, 5:6], 0.0), ins=[bf("mcol")],
               outs=[bf("mcol")])
        V.emit(lambda e: e.memset(mcol[:, 1:2], 0.0), ins=[bf("mcol")],
               outs=[bf("mcol")])
        tailq = []
        for tc in range(NT):
            for sg in range(6):
                it = tc * 6 + sg
                gp = it % 4 == 3
                p = 1 if gp else 0
                E = V
                if gp:
                    for _t in tailq:
                        _t()
                    tailq = []
                acc = 1 if p else 5
                tmp = 6 if p else 7
                ustg, gstg, ostg, g1, mt = stg[p]
                load(ustg, upraw_d.ap()[sg * 4:(sg + 1) * 4, :,
                                        tc * T:(tc + 1) * T]
                     .rearrange("c p t -> p c t"),
                     outs=[bf(("ustgG", p))],
                     ins=[bf(("upr", sg * 4 + i, tc)) for i in range(4)])
                load(gstg, g2raw_d.ap()[sg * 4:(sg + 1) * 4, :,
                                        tc * T:(tc + 1) * T]
                     .rearrange("c p t -> p c t"),
                     outs=[bf(("gstgG", p))],
                     ins=[bf(("g2r", sg * 4 + i, tc)) for i in range(4)])
                if not gp:
                    E.emit(lambda e, g1=g1, ustg=ustg: e.tensor_scalar(
                        g1, ustg, rcols[:, 3:4], None, OP.mult),
                        ins=[bf(("ustgG", p)), bf("rcols")], outs=[bf(("g1g", p))])
                    rne_e(E, g1, bf(("g1g", p)), ustg, bf(("ustgG", p)),
                          mt, bf(("mtg", p)), ostg, bf(("ostgG", p)))
                    E.emit(lambda e, g1=g1, gstg=gstg: e.tensor_scalar(
                        g1, gstg, rcols[:, 7:8], None, OP.mult),
                        ins=[bf(("gstgG", p)), bf("rcols")], outs=[bf(("g1g", p))])
                    rne_e(E, g1, bf(("g1g", p)), gstg, bf(("gstgG", p)),
                          mt, bf(("mtg", p)), ostg, bf(("ostgG", p)))
                    V.emit(lambda e, ostg=ostg, ustg=ustg, gstg=gstg:
                           e.scalar_tensor_tensor(
                               ostg, ustg, rcols[:, 11:12], gstg,
                               OP.mult, OP.mult),
                           ins=[bf(("ustgG", p)), bf(("gstgG", p)), bf("rcols")],
                           outs=[bf(("ostgG", p))])
                else:
                    bctt(g1, ustg, rcols[:, 3:4], OP.mult,
                         [bf(("ustgG", p)), bf("rcols")], [bf(("g1g", p))], W4)
                    rne_g(g1, bf(("g1g", p)), ustg, bf(("ustgG", p)),
                          mt, bf(("mtg", p)), ostg, bf(("ostgG", p)), W4)
                    bctt(g1, gstg, rcols[:, 7:8], OP.mult,
                         [bf(("gstgG", p)), bf("rcols")], [bf(("g1g", p))], W4)
                    rne_g(g1, bf(("g1g", p)), gstg, bf(("gstgG", p)),
                          mt, bf(("mtg", p)), ostg, bf(("ostgG", p)), W4)
                    bctt(mt, ustg, rcols[:, 11:12], OP.mult,
                         [bf(("ustgG", p)), bf("rcols")], [bf(("mtg", p))], W4)
                    gtt(ostg, mt, gstg, OP.mult,
                        [bf(("mtg", p)), bf(("gstgG", p))], [bf(("ostgG", p))])

                def tail(p=p, ostg=ostg, acc=acc, tmp=tmp, sg=sg, tc=tc):
                    V.emit(lambda e: e.tensor_reduce(
                        mcol[:, tmp:tmp + 1], ostg, AX, OP.max,
                        apply_absolute_value=True),
                        ins=[bf(("ostgG", p))], outs=[bf("mcol")])
                    V.emit(lambda e: e.tensor_tensor(
                        mcol[:, acc:acc + 1], mcol[:, acc:acc + 1],
                        mcol[:, tmp:tmp + 1], OP.max),
                        ins=[bf("mcol")], outs=[bf("mcol")])
                    store(oraw_d.ap()[sg * 4:(sg + 1) * 4, :,
                                      tc * T:(tc + 1) * T]
                          .rearrange("c p t -> p c t"), ostg,
                          ins=[bf(("ostgG", p))],
                          outs=[bf(("ord", sg * 4 + i, tc)) for i in range(4)])
                if gp:
                    tailq.append(tail)
                else:
                    tail()
        for _t in tailq:
            _t()
        V.emit(lambda e: e.tensor_tensor(mcol[:, 5:6], mcol[:, 5:6],
                                         mcol[:, 1:2], OP.max),
               ins=[bf("mcol")], outs=[bf("mcol")])
        cross_max_to_cc(5, "5", 0)
        cc_run("5", 1)
        scalars_to_cols("5", [(0, 8, 9)])
        V.emit(lambda e: e.tensor_scalar(modc2[:, 2 * KD:3 * KD],
                                         modc[:, 5 * KD:6 * KD],
                                         rcols[:, 8:9], None, OP.mult),
               ins=[bf("modc"), bf("rcols")], outs=[bf("modc2")])
        barrier()

    g_phase()
    if upto == "G":
        es.close()
        return nc

    def h_quant():
        W4 = 4 * T
        stg = {}
        bstg = {}
        for p in range(2):
            b0 = p * 5 * W4
            stg[p] = (fsl(b0, W4), fsl(b0 + W4, W4), fsl(b0 + 2 * W4, W4),
                      fsl(b0 + 3 * W4, W4), fsl(b0 + 4 * W4, W4))
            bstg[p] = (bsl(p * 2 * W4, W4), bsl(p * 2 * W4 + W4, W4))
        tailq = []
        for tc in range(NT):
            for sg in range(6):
                it = tc * 6 + sg
                gp = it % 4 == 3
                p = 1 if gp else 0
                E = V
                if gp:
                    for _t in tailq:
                        _t()
                    tailq = []
                ostg, g1, mt, ro, t4 = stg[p]
                roh, rol = bstg[p]
                load(ostg, oraw_d.ap()[sg * 4:(sg + 1) * 4, :,
                                       tc * T:(tc + 1) * T]
                     .rearrange("c p t -> p c t"),
                     outs=[bf(("ostgH", p))],
                     ins=[bf(("ord", sg * 4 + i, tc)) for i in range(4)])
                if not gp:
                    E.emit(lambda e, g1=g1, ostg=ostg: e.tensor_scalar(
                        g1, ostg, rcols[:, 9:10], None, OP.mult),
                        ins=[bf(("ostgH", p)), bf("rcols")], outs=[bf(("g1h", p))])
                    rne_e(E, g1, bf(("g1h", p)), ro, bf(("roh2", p)),
                          mt, bf(("mth2", p)), ostg, bf(("ostgH", p)))
                    split256_e(E, ro, bf(("roh2", p)), roh, bf(("rohxx", p)),
                               rol, bf(("rolxx", p)), t4, bf(("t4h", p)))
                else:
                    bctt(g1, ostg, rcols[:, 9:10], OP.mult,
                         [bf(("ostgH", p)), bf("rcols")], [bf(("g1h", p))], W4)
                    rne_g(g1, bf(("g1h", p)), ro, bf(("roh2", p)),
                          mt, bf(("mth2", p)), ostg, bf(("ostgH", p)), W4)
                    split256_g(ro, bf(("roh2", p)), roh, bf(("rohxx", p)),
                               rol, bf(("rolxx", p)), t4, bf(("t4h", p)), W4)
                def tail(p=p, roh=roh, rol=rol, sg=sg, tc=tc):
                    store(roh_d.ap()[sg * 4:(sg + 1) * 4, :,
                                     tc * T:(tc + 1) * T]
                          .rearrange("c p t -> p c t"), roh,
                          ins=[bf(("rohxx", p))],
                          outs=[bf(("rohd", sg * 4 + i, tc)) for i in range(4)])
                    store(rol_d.ap()[sg * 4:(sg + 1) * 4, :,
                                     tc * T:(tc + 1) * T]
                          .rearrange("c p t -> p c t"), rol,
                          ins=[bf(("rolxx", p))],
                          outs=[bf(("rold", sg * 4 + i, tc)) for i in range(4)])
                if gp:
                    tailq.append(tail)
                else:
                    tail()
        for _t in tailq:
            _t()
        barrier()

    def h_down():
        ro_b = 2 * KI3 * 256
        hb0 = 4 * I3 + I3 // 32 + 256
        x2sv = [fsl(hb0, 2 * T), fsl(hb0 + 2 * T, 2 * T)]
        finv = [fsl(hb0 + 4 * T, 2 * T), fsl(hb0 + 6 * T, 2 * T)]
        trsv = [fsl(hb0 + 8 * T, 2 * T), fsl(hb0 + 10 * T, 2 * T)]
        for wg in range(4):
            lpbq_group(w_down_in, "wD", wg, 2)
            for tc in range(NT):
                mb = 2 * (tc % 2)
                x2s = x2sv[tc % 2]
                fin = finv[tc % 2]
                trs = trsv[tc % 2]
                load(x2s, x2_d.ap()[wg * 2:(wg + 1) * 2, :, tc * T:(tc + 1) * T]
                     .rearrange("c p t -> p c t"),
                     outs=[bf(("x2s", tc % 2))], ins=[bf(("x2d", wg // 2, tc))])
                for q in range(4):
                    sl = q % 2
                    load(bsl(ro_b + sl * 12 * T, 6 * T),
                         roh_d.ap()[q * 6:(q + 1) * 6, :, tc * T:(tc + 1) * T]
                         .rearrange("c p t -> p c t"),
                         outs=[bf(("r4h", sl))],
                         ins=[bf(("rohd", q * 6 + k, tc)) for k in range(6)])
                    load(bsl(ro_b + sl * 12 * T + 6 * T, 6 * T),
                         rol_d.ap()[q * 6:(q + 1) * 6, :, tc * T:(tc + 1) * T]
                         .rearrange("c p t -> p c t"),
                         outs=[bf(("r4l", sl))],
                         ins=[bf(("rold", q * 6 + k, tc)) for k in range(6)])
                    for og1 in range(2):
                        mms = []
                        for k in range(6):
                            kk = q * 6 + k
                            lh = bsl(kk * 256 + og1 * 128, 128)
                            ll = bsl(KI3 * 256 + kk * 256 + og1 * 128, 128)
                            rh = bsl(ro_b + sl * 12 * T + k * T, T)
                            rl = bsl(ro_b + sl * 12 * T + 6 * T + k * T, T)
                            mms += [(pbank[mb + og1][:, :T], lh, rh,
                                     q == 0 and k == 0, False),
                                    (pbank[mb + og1][:, :T], lh, rl,
                                     False, False),
                                    (pbank[mb + og1][:, :T], ll, rh, False,
                                     q == 3 and k == 5)]
                        mmchain(("pb", mb + og1), mms,
                                [bf(("wD", "h", q * 6 + k)) for k in range(6)]
                                + [bf(("wD", "l", q * 6 + k)) for k in range(6)]
                                + [bf(("r4h", sl)), bf(("r4l", sl))])
                for og1 in range(2):
                    oc = wg * 2 + og1
                    fsl_o = fin[:, og1 * T:(og1 + 1) * T]
                    V.emit(lambda e, fsl_o=fsl_o, x2s=x2s, og1=og1, oc=oc, mb=mb:
                           e.scalar_tensor_tensor(
                        fsl_o, pbank[mb + og1][:, :T],
                        modc2[:, 2 * KD + oc:2 * KD + oc + 1],
                        x2s[:, og1 * T:(og1 + 1) * T], OP.mult, OP.add),
                        ins=[bf(("pb", mb + og1)), bf("modc2"),
                             bf(("x2s", tc % 2))],
                        outs=[bf(("finh", tc % 2, og1))])
                    tb = 4 + og1
                    trchain(("pb", tb),
                            [(pbank[tb][:, j * 128:(j + 1) * 128],
                              fsl_o[:, j * 128:(j + 1) * 128])
                             for j in range(JT)],
                            [bf(("finh", tc % 2, og1))])
                    tr_o = trs[:, og1 * T:(og1 + 1) * T]
                    V.emit(lambda e, tb=tb, tr_o=tr_o: e.tensor_copy(
                        tr_o, pbank[tb][:, 0:JT * 128]),
                        ins=[bf(("pb", tb))], outs=[bf(("trsb", tc % 2, og1))])
                    store(out_t.ap()[tc * T:(tc + 1) * T, oc * 128:(oc + 1) * 128]
                          .rearrange("(j p) f -> p j f", p=128),
                          tr_o.rearrange("p (j f) -> p j f", j=JT),
                          ins=[bf(("trsb", tc % 2, og1))],
                          outs=[bf(("outd", oc, tc))])
            barrier()

    h_quant()
    h_down()

    for _e in ENGS:
        _e.flush()
    fin_toks = [bf(("outd", og, tc)).w for og in range(8) for tc in range(NT)]
    best = {}
    for t in fin_toks:
        kk = id(t.sem)
        if kk not in best or best[kk].cnt < t.cnt:
            best[kk] = t

    def fin_body(eng):
        for t in best.values():
            eng.wait_ge(t.sem, t.cnt)

    blk.sync(fin_body)
    blk.gpsimd(fin_body)

    es.close()
    return nc


def _prep_inputs(inputs):
    import ml_dtypes
    asc = np.ascontiguousarray
    f32 = np.float32
    w = {}
    w["w_in_T"] = asc(np.asarray(inputs["L0_attn_in_proj_w"], dtype=f32).T)
    w["conv_w_in"] = asc(np.asarray(inputs["L0_attn_conv_w"], dtype=f32))
    w["conv_b_in"] = asc(np.asarray(inputs["L0_attn_conv_b"], dtype=f32))
    w["w_xproj_T"] = asc(np.asarray(inputs["L0_attn_x_proj_w"], dtype=f32).T)
    w["w_dt_T"] = asc(np.asarray(inputs["L0_attn_dt_proj_w"], dtype=f32).T)
    w["dt_b_in"] = asc(np.asarray(inputs["L0_attn_dt_proj_b"], dtype=f32))
    w["a_log_in"] = asc(np.asarray(inputs["L0_attn_A_log"], dtype=f32))
    w["d_par_in"] = asc(np.asarray(inputs["L0_attn_D_param"], dtype=f32))
    w["w_out_T"] = asc(np.asarray(inputs["L0_attn_out_proj_w"], dtype=f32).T)
    w["w_gate_in"] = asc(np.asarray(inputs["L0_ffn_w_gate"], dtype=f32))
    w["w_up_in"] = asc(np.asarray(inputs["L0_ffn_w_up"], dtype=f32))
    w["w_down_in"] = asc(np.asarray(inputs["L0_ffn_w_down"], dtype=f32))
    w["wmod_in"] = asc(np.asarray(inputs["L0_wmod"], dtype=f32))
    w["eye_in"] = np.eye(128, dtype=f32)
    w["consts_in"] = np.tile(np.array([1e-5, 1.0, 0.5, -1.0, -0.5, 65536.0,
                                       2.0, 0.25, M52, -M52, M32, -M32,
                                       M45, -M45, 65535.0, 0.00390625,
                                       256.0, 0.0, 0.0, 0.0], dtype=f32),
                             (128, 1))
    sel32 = np.zeros((32, 32 * 128), dtype=ml_dtypes.bfloat16)
    for n in range(32):
        sel32[n, n * 128:(n + 1) * 128] = 1.0
    w["sel32_in"] = sel32
    sel4 = np.zeros((4, 4 * 128), dtype=f32)
    for n in range(4):
        sel4[n, n * 128:(n + 1) * 128] = 1.0
    w["sel4_in"] = sel4
    x = asc(np.asarray(inputs["L0_attn_hidden_states"], dtype=f32))
    c = asc(np.asarray(inputs["L0_c"], dtype=f32))
    maps = []
    for b in range(NCORE):
        m = dict(w)
        m["x_in"] = asc(x[b])
        m["c_in"] = asc(c[b:b + 1])
        maps.append(m)
    return maps


def kernel(**inputs):
    L = int(np.asarray(inputs["L0_attn_hidden_states"]).shape[1])
    T = 512 if (L % 512 == 0 and L >= 2048) else max(128, L // 4)
    nc = build_nc(L, T)
    in_maps = _prep_inputs(inputs)
    res = run_bass_kernel_spmd(nc, in_maps, core_ids=list(range(NCORE)))
    out = np.stack([np.asarray(r["out"]) for r in res.results], axis=0)
    return np.ascontiguousarray(out.astype(np.float32))


# version 18
# speedup vs baseline: 1.0433x; 1.0433x over previous
import contextlib

import numpy as np

import concourse.bass as bass
import concourse.mybir as mybir
from concourse.alu_op_type import AluOpType as OP
from concourse.bass_utils import run_bass_kernel_spmd
from concourse.replica_groups import maybe_share_collective_output_space

F32 = mybir.dt.float32
BF16 = mybir.dt.bfloat16
AF = mybir.ActivationFunctionType
AX = mybir.AxisListType.X

NCORE = 8
D = 1024
DI = 2048
NS = 16
RK = 64
XD = 96
I3 = 3072
QM = 32767.0
EPS = 1e-5
M32 = 12582912.0
M52 = 4503599627370496.0
M45 = 35184372088832.0
CONST_VALS = [1e-5, 1.0, 0.5, -1.0, -0.5, 65536.0, 2.0, 0.25]
KD = D // 128
KDI = DI // 128
KI3 = I3 // 128


class Tok:
    __slots__ = ("sem", "cnt")

    def __init__(self, sem, cnt):
        self.sem = sem
        self.cnt = cnt


class Buf:
    __slots__ = ("w", "r")

    def __init__(self):
        self.w = None
        self.r = []


class DmaSem:
    __slots__ = ("h", "cnt")

    def __init__(self, h):
        self.h = h
        self.cnt = 0


class Eng:
    MAX_PENDING = 8000

    def __init__(self, section, sem):
        self.section = section
        self.sem = sem
        self.cnt = 0
        self.waited = {}
        self.pending = []

    def flush(self):
        if not self.pending:
            return
        p = self.pending
        self.pending = []

        def body(eng):
            for r in p:
                r(eng)

        self.section(body)

    def _collect(self, ins, outs, deps):
        toks = []
        for b in ins:
            if b.w is not None:
                toks.append(b.w)
        for b in outs:
            if b.w is not None:
                toks.append(b.w)
            toks.extend(b.r)
        for t in deps:
            if t is not None:
                toks.append(t)
        need = {}
        for t in toks:
            k = id(t.sem)
            if self.waited.get(k, 0) < t.cnt and need.get(k, (None, 0))[1] < t.cnt:
                need[k] = (t.sem, t.cnt)
        for k, (s, c) in need.items():
            self.waited[k] = c
        return list(need.values())

    def emit(self, fn, ins=(), outs=(), deps=(), dma=None):
        return self.emit_multi(fn, ins, outs, deps, dma)

    def emit_multi(self, body, ins=(), outs=(), deps=(), dma=None):
        need = self._collect(ins, outs, deps)
        if dma is None:
            self.cnt += 1
            sem_h, cnt, step = self.sem, self.cnt, 1
        else:
            dma.cnt += 16
            sem_h, cnt, step = dma.h, dma.cnt, 16

        def run(eng):
            for s, c in need:
                eng.wait_ge(s, c)
            body(eng).then_inc(sem_h, step)

        self.pending.append(run)
        if len(self.pending) >= self.MAX_PENDING:
            self.flush()
        tok = Tok(sem_h, cnt)
        for b in ins:
            b.r.append(tok)
        for b in outs:
            b.w = tok
            b.r = []
        return tok


def build_nc(L, T, upto="all", ncc=NCORE):
    NT = L // T
    JT = T // 128
    LP = L // 128
    nc = bass.Bass("TRN2", target_bir_lowering=False)

    x_in = nc.dram_tensor("x_in", [L, D], F32, kind="ExternalInput")
    c_in = nc.dram_tensor("c_in", [1, D], F32, kind="ExternalInput")
    w_in_T = nc.dram_tensor("w_in_T", [D, 2 * DI], F32, kind="ExternalInput")
    conv_w_in = nc.dram_tensor("conv_w_in", [DI, 4], F32, kind="ExternalInput")
    conv_b_in = nc.dram_tensor("conv_b_in", [DI], F32, kind="ExternalInput")
    w_xproj_T = nc.dram_tensor("w_xproj_T", [DI, XD], F32, kind="ExternalInput")
    w_dt_T = nc.dram_tensor("w_dt_T", [RK, DI], F32, kind="ExternalInput")
    dt_b_in = nc.dram_tensor("dt_b_in", [DI], F32, kind="ExternalInput")
    a_log_in = nc.dram_tensor("a_log_in", [DI, NS], F32, kind="ExternalInput")
    d_par_in = nc.dram_tensor("d_par_in", [DI], F32, kind="ExternalInput")
    w_out_T = nc.dram_tensor("w_out_T", [DI, D], F32, kind="ExternalInput")
    w_gate_in = nc.dram_tensor("w_gate_in", [I3, D], F32, kind="ExternalInput")
    w_up_in = nc.dram_tensor("w_up_in", [I3, D], F32, kind="ExternalInput")
    w_down_in = nc.dram_tensor("w_down_in", [D, I3], F32, kind="ExternalInput")
    wmod_in = nc.dram_tensor("wmod_in", [D, 6 * D], F32, kind="ExternalInput")
    eye_in = nc.dram_tensor("eye_in", [128, 128], F32, kind="ExternalInput")
    sel32_in = nc.dram_tensor("sel32_in", [32, 32 * 128], BF16, kind="ExternalInput")
    sel4_in = nc.dram_tensor("sel4_in", [4, 4 * 128], F32, kind="ExternalInput")
    consts_in = nc.dram_tensor("consts_in", [128, 20], F32, kind="ExternalInput")
    out_t = nc.dram_tensor("out", [L, D], F32, kind="ExternalOutput")

    x_fm_d = nc.dram_tensor("x_fm_d", [KD, 128, L], F32)
    h_hi_d = nc.dram_tensor("h_hi_d", [KD, 128, L], BF16)
    h_lo_d = nc.dram_tensor("h_lo_d", [KD, 128, L], BF16)
    z_d = nc.dram_tensor("z_d", [KDI, 128, L], F32)
    u_d = nc.dram_tensor("u_d", [KDI, 128, L], F32)
    y_d = nc.dram_tensor("y_d", [KDI, 128, L], F32)
    ygh_d = nc.dram_tensor("ygh_d", [KDI, 128, L], BF16)
    ygl_d = nc.dram_tensor("ygl_d", [KDI, 128, L], BF16)
    xdbl_d = nc.dram_tensor("xdbl_d", [XD, L], F32)
    x2_d = nc.dram_tensor("x2_d", [KD, 128, L], F32)
    k1h_d = nc.dram_tensor("k1h_d", [KD, 128, L], BF16)
    k1l_d = nc.dram_tensor("k1l_d", [KD, 128, L], BF16)
    upraw_d = nc.dram_tensor("upraw_d", [KI3, 128, L], F32)
    gateraw_d = nc.dram_tensor("gateraw_d", [KI3, 128, L], F32)
    g2raw_d = nc.dram_tensor("g2raw_d", [KI3, 128, L], F32)
    oraw_d = nc.dram_tensor("oraw_d", [KI3, 128, L], F32)
    roh_d = nc.dram_tensor("roh_d", [KI3, 128, L], BF16)
    rol_d = nc.dram_tensor("rol_d", [KI3, 128, L], BF16)
    rows_d = nc.dram_tensor("rows_d", [8, L], F32)
    mod_d = nc.dram_tensor("mod_d", [1, 6 * D], F32)
    colb_d = nc.dram_tensor("colb_d", [128, 1], F32)
    core_ids = list(range(ncc))
    spc = maybe_share_collective_output_space("AllReduce", [core_ids])
    cc_i = {k: nc.dram_tensor("cc_i%s" % k, [1, w], F32)
            for k, w in (("1", 1), ("23", 2), ("4", 1), ("5", 1))}
    cc_o = {k: nc.dram_tensor("cc_o%s" % k, [1, w], F32, addr_space=spc)
            for k, w in (("1", 1), ("23", 2), ("4", 1), ("5", 1))}

    es = contextlib.ExitStack()

    def sbt(name, shape, dt=F32):
        return es.enter_context(nc.sbuf_tensor(name, shape, dt))

    eye = sbt("eye", [128, 128])
    consts = sbt("consts", [128, 20])
    sel4 = sbt("sel4", [4, 4 * 128])
    sel32 = sbt("sel32", [32, 32 * 128], BF16)
    ones_row = sbt("ones_row", [1, 128])
    ones_col = sbt("ones_col", [128, 1])
    a_neg = sbt("a_neg", [128, KDI * NS])
    conv_w = sbt("conv_w", [128, KDI * 4])
    conv_b = sbt("conv_b", [128, KDI])
    dt_b = sbt("dt_b", [128, KDI])
    d_par = sbt("d_par", [128, KDI])
    dtw_h = sbt("dtw_h", [RK, DI], BF16)
    dtw_l = sbt("dtw_l", [RK, DI], BF16)
    xpj_h = sbt("xpj_h", [128, KDI * XD], BF16)
    xpj_l = sbt("xpj_l", [128, KDI * XD], BF16)
    modc = sbt("modc", [128, 6 * KD])
    modc2 = sbt("modc2", [128, 3 * KD])
    rcols = sbt("rcols", [128, 16])
    stat_st = sbt("stat_st", [4, T])
    bc_h = sbt("bc_h", [32, T], BF16)
    bc_l = sbt("bc_l", [32, T], BF16)
    bc_f = sbt("bc_f", [32, T])
    row_st = sbt("row_st", [1, T])
    row_st2 = sbt("row_st2", [1, T])
    r128 = sbt("r128", [1, 128])
    r1t = sbt("r1t", [1, 2])
    s64a = sbt("s64a", [128, 4 * LP])
    halo = sbt("halo", [128, KDI * 4])
    mcol = sbt("mcol", [128, 8])

    FW = 20500
    BW = 33792
    f32a = sbt("f32a", [128, FW])
    bfa = sbt("bfa", [128, BW], BF16)

    pbank = [es.enter_context(nc.psum_tensor("pb%d" % i, [128, 512], F32))
             for i in range(8)]

    esem = {n: es.enter_context(nc.semaphore(n)) for n in
            ["sv", "ss", "st", "sg", "ssp", "bar"]}
    dpool = [DmaSem(es.enter_context(nc.semaphore("dm%d" % i))) for i in range(48)]

    blk = es.enter_context(nc.Block())
    V = Eng(blk.vector, esem["sv"])
    S = Eng(blk.scalar, esem["ss"])
    TE = Eng(blk.tensor, esem["st"])
    G = Eng(blk.gpsimd, esem["sg"])
    SP = Eng(blk.sync, esem["ssp"])
    ENGS = [V, S, TE, G, SP]

    BUFS = {}

    def bf(key):
        if key not in BUFS:
            BUFS[key] = Buf()
        return BUFS[key]

    _di = [0]

    def dsem():
        s = dpool[_di[0] % len(dpool)]
        _di[0] += 1
        return s

    def load(dst, src, outs, ins=(), deps=(), slow=False):
        return SP.emit(lambda e: e.dma_start(out=dst, in_=src,
                                             allow_slow_non_contiguous=slow),
                       ins=ins, outs=outs, deps=deps, dma=dsem())

    def store(dst, src, ins, outs=(), deps=(), accum=False):
        if accum:
            return SP.emit(lambda e: e.dma_start(out=dst, in_=src, accum_op=OP.add),
                           ins=ins, outs=outs, deps=deps, dma=dsem())
        return SP.emit(lambda e: e.dma_start(out=dst, in_=src),
                       ins=ins, outs=outs, deps=deps, dma=dsem())

    def fsl(off, w, p=128, p0=0):
        assert off + w <= FW, (off, w)
        return f32a[p0:p0 + p, off:off + w]

    def bsl(off, w, p=128, p0=0):
        assert off + w <= BW, (off, w)
        return bfa[p0:p0 + p, off:off + w]

    _bar = [0]

    def barrier():
        for e in ENGS:
            e.flush()
        waits = [(e.sem, e.cnt) for e in ENGS] + [(d.h, d.cnt) for d in dpool]
        waits = [(s, c) for (s, c) in waits if c > 0]
        _bar[0] += 1
        bval = _bar[0]

        def spf(e):
            for s, c in waits:
                e.wait_ge(s, c)
            e.nop().then_inc(esem["bar"], 1)

        blk.sync(spf)
        for eng in (V, S, TE, G):
            eng.section(lambda e: e.wait_ge(esem["bar"], bval))
        for eng in ENGS:
            for s, c in waits:
                eng.waited[id(s)] = max(eng.waited.get(id(s), 0), c)
            eng.waited[id(esem["bar"])] = bval

    def split_pair(src_ap, hi_ap, lo_ap, src_b, hi_b, lo_b, eng=None):
        eng = eng or V
        eng.emit(lambda e: e.tensor_copy(hi_ap, src_ap), ins=[src_b], outs=[hi_b])
        eng.emit(lambda e: e.tensor_tensor(lo_ap, src_ap, hi_ap, OP.subtract),
                 ins=[src_b, hi_b], outs=[lo_b])

    def rne(y_ap, y_b, r_ap, r_b, t1_ap, t1_b, t2_ap, t2_b):
        return rne_e(V, y_ap, y_b, r_ap, r_b, t1_ap, t1_b, t2_ap, t2_b)

    def rne_e(E, y_ap, y_b, r_ap, r_b, t1_ap, t1_b, t2_ap, t2_b):
        E.emit(lambda e: e.tensor_scalar(t1_ap, y_ap, M52, -M52, OP.add, OP.add),
               ins=[y_b], outs=[t1_b])
        E.emit(lambda e: e.tensor_tensor(t2_ap, y_ap, t1_ap, OP.subtract),
               ins=[y_b, t1_b], outs=[t2_b])
        E.emit(lambda e: e.tensor_scalar(r_ap, t2_ap, M32, -M32, OP.add, OP.add),
               ins=[t2_b], outs=[r_b])
        E.emit(lambda e: e.tensor_scalar(t2_ap, t2_ap, M45, -M45, OP.add, OP.add),
               ins=[t2_b], outs=[t2_b])
        E.emit(lambda e: e.tensor_tensor(r_ap, r_ap, t2_ap, OP.subtract),
               ins=[r_b, t2_b], outs=[r_b])
        E.emit(lambda e: e.tensor_tensor(r_ap, r_ap, t1_ap, OP.add),
               ins=[r_b, t1_b], outs=[r_b])
        return r_b

    def split256(r_ap, r_b, hi_ap, hi_b, lo_ap, lo_b, t_ap, t_b):
        return split256_e(V, r_ap, r_b, hi_ap, hi_b, lo_ap, lo_b, t_ap, t_b)

    def split256_e(E, r_ap, r_b, hi_ap, hi_b, lo_ap, lo_b, t_ap, t_b):
        E.emit(lambda e: e.tensor_scalar(t_ap, r_ap, 0.00390625, M32,
                                         OP.mult, OP.add),
               ins=[r_b], outs=[t_b])
        E.emit(lambda e: e.tensor_scalar(hi_ap, t_ap, -M32, 256.0,
                                         OP.add, OP.mult),
               ins=[t_b], outs=[hi_b])
        E.emit(lambda e: e.tensor_tensor(lo_ap, r_ap, hi_ap, OP.subtract),
               ins=[r_b, hi_b], outs=[lo_b])

    def gtt(dst, a, b, op, ins, outs):
        G.emit(lambda e: e.tensor_tensor(dst, a, b, op), ins=ins, outs=outs)

    def bctt(dst, a, col, op, ins, outs, w):
        d3 = dst.rearrange("p (c t) -> p c t", t=1)
        a3 = a.rearrange("p (c t) -> p c t", t=1)
        b3 = col.unsqueeze(1).broadcast_to([128, w, 1])
        G.emit(lambda e: e.tensor_tensor(d3, a3, b3, op), ins=ins, outs=outs)

    def rne_g(y_ap, y_b, r_ap, r_b, t1_ap, t1_b, t2_ap, t2_b, w):
        cb_ = bf("consts")
        cj = lambda j: consts[:, j:j + 1]
        bctt(t1_ap, y_ap, cj(8), OP.add, [y_b, cb_], [t1_b], w)
        bctt(t1_ap, t1_ap, cj(9), OP.add, [t1_b, cb_], [t1_b], w)
        gtt(t2_ap, y_ap, t1_ap, OP.subtract, [y_b, t1_b], [t2_b])
        bctt(r_ap, t2_ap, cj(10), OP.add, [t2_b, cb_], [r_b], w)
        bctt(r_ap, r_ap, cj(11), OP.add, [r_b, cb_], [r_b], w)
        bctt(t2_ap, t2_ap, cj(12), OP.add, [t2_b, cb_], [t2_b], w)
        bctt(t2_ap, t2_ap, cj(13), OP.add, [t2_b, cb_], [t2_b], w)
        gtt(r_ap, r_ap, t2_ap, OP.subtract, [r_b, t2_b], [r_b])
        gtt(r_ap, r_ap, t1_ap, OP.add, [r_b, t1_b], [r_b])
        return r_b

    def split256_g(r_ap, r_b, hi_ap, hi_b, lo_ap, lo_b, t_ap, t_b, w):
        cb_ = bf("consts")
        cj = lambda j: consts[:, j:j + 1]
        bctt(t_ap, r_ap, cj(15), OP.mult, [r_b, cb_], [t_b], w)
        bctt(t_ap, t_ap, cj(10), OP.add, [t_b, cb_], [t_b], w)
        bctt(t_ap, t_ap, cj(11), OP.add, [t_b, cb_], [t_b], w)
        bctt(hi_ap, t_ap, cj(16), OP.mult, [t_b, cb_], [hi_b], w)
        gtt(lo_ap, r_ap, hi_ap, OP.subtract, [r_b, hi_b], [lo_b])

    def mmchain(bank_key, mms, dep_bufs):
        bank = bf(bank_key)

        def body(e):
            last = None
            for (pa, lh, rh, st_, sp_) in mms:
                last = e.matmul(pa, lh, rh, start=st_, stop=sp_)
            return last

        return TE.emit_multi(body, ins=list(dep_bufs), outs=[bank])

    def trchain(bank_key, trs, dep_bufs):
        bank = bf(bank_key)

        def body(e):
            last = None
            for (pa, ia) in trs:
                last = e.transpose(pa, ia, eye[:, :])
            return last

        return TE.emit_multi(body, ins=list(dep_bufs) + [bf("eye")], outs=[bank])

    load(eye[:, :], eye_in.ap(), outs=[bf("eye")])
    load(consts[:, :], consts_in.ap(), outs=[bf("consts")])
    for j, cv in enumerate(CONST_VALS):
        nc.const_aps.register(F32, cv, consts[:, j:j + 1])
    load(sel4[:, :], sel4_in.ap(), outs=[bf("sel4")])
    load(sel32[:, :], sel32_in.ap(), outs=[bf("sel32")])
    V.emit(lambda e: e.memset(ones_row[:, :], 1.0), outs=[bf("ones_row")])
    V.emit(lambda e: e.memset(ones_col[:, :], 1.0), outs=[bf("ones_col")])
    htok = V.emit(lambda e: e.memset(halo[:, :], 0.0), outs=[bf("halo_init")])
    V.emit(lambda e: e.memset(stat_st[:, :], 0.0), outs=[bf("stat_st")])
    for hcc in range(KDI):
        bf(("halo", hcc)).w = htok
    load(a_neg[:, :], a_log_in.ap().rearrange("(c p) n -> p c n", p=128),
         outs=[bf("a_neg")])
    S.emit(lambda e: e.activation(a_neg[:, :], a_neg[:, :], AF.Exp),
           ins=[bf("a_neg")], outs=[bf("a_neg")])
    V.emit(lambda e: e.tensor_scalar_mul(a_neg[:, :], a_neg[:, :], -1.0),
           ins=[bf("a_neg")], outs=[bf("a_neg")])
    load(conv_w[:, :], conv_w_in.ap().rearrange("(c p) w -> p c w", p=128),
         outs=[bf("conv_w")])
    load(conv_b[:, :], conv_b_in.ap().rearrange("(c p) -> p c", p=128),
         outs=[bf("conv_b")], slow=True)
    load(dt_b[:, :], dt_b_in.ap().rearrange("(c p) -> p c", p=128),
         outs=[bf("dt_b")], slow=True)
    load(d_par[:, :], d_par_in.ap().rearrange("(c p) -> p c", p=128),
         outs=[bf("d_par")], slow=True)
    dtw_f = fsl(0, DI, p=RK)
    load(dtw_f, w_dt_T.ap(), outs=[bf("dtw_f")])
    split_pair(dtw_f, dtw_h[:, :], dtw_l[:, :], bf("dtw_f"), bf("dtw_h"), bf("dtw_l"))
    xpj_f = fsl(2048, KDI * XD)
    load(xpj_f, w_xproj_T.ap().rearrange("(k p) o -> p k o", p=128),
         outs=[bf("xpj_f")])
    split_pair(xpj_f, xpj_h[:, :], xpj_l[:, :], bf("xpj_f"), bf("xpj_h"), bf("xpj_l"))

    c_fm = fsl(4000, KD)
    load(c_fm, c_in.ap().rearrange("o (k p) -> p o k", p=128), outs=[bf("c_fm")], slow=True)
    c_sg = fsl(4100, KD)
    S.emit(lambda e: e.activation(c_sg, c_fm, AF.Sigmoid), ins=[bf("c_fm")],
           outs=[bf("c_sg")])
    c_sl = fsl(4200, KD)
    V.emit(lambda e: e.tensor_tensor(c_sl, c_fm, c_sg, OP.mult),
           ins=[bf("c_fm"), bf("c_sg")], outs=[bf("c_sl")])
    sc_h, sc_l = bsl(0, KD), bsl(KD, KD)
    split_pair(c_sl, sc_h, sc_l, bf("c_sl"), bf("sc_h"), bf("sc_l"))
    WMW = 6 * D
    wmf = fsl(4400, WMW)
    modrow = fsl(11000, WMW, p=1)
    wm_h = bsl(2 * KD, WMW)
    wm_l = bsl(2 * KD + WMW, WMW)
    for k in range(KD):
        load(wmf, wmod_in.ap()[k * 128:(k + 1) * 128, :], outs=[bf("wmf")])
        split_pair(wmf, wm_h, wm_l, bf("wmf"), bf("wm_h"), bf("wm_l"))
        for piece in range(WMW // 512):
            mms = [(pbank[0][0:1, :], sc_h[:, k:k + 1],
                    wm_h[:, piece * 512:(piece + 1) * 512], True, False),
                   (pbank[0][0:1, :], sc_h[:, k:k + 1],
                    wm_l[:, piece * 512:(piece + 1) * 512], False, False),
                   (pbank[0][0:1, :], sc_l[:, k:k + 1],
                    wm_h[:, piece * 512:(piece + 1) * 512], False, True)]
            mmchain(("pb", 0), mms, [bf("sc_h"), bf("sc_l"), bf("wm_h"), bf("wm_l")])
            mr = modrow[:, piece * 512:(piece + 1) * 512]
            if k == 0:
                V.emit(lambda e, mr=mr: e.tensor_copy(mr, pbank[0][0:1, :]),
                       ins=[bf(("pb", 0))], outs=[bf("modrow")])
            else:
                V.emit(lambda e, mr=mr: e.tensor_tensor(mr, mr, pbank[0][0:1, :],
                                                        OP.add),
                       ins=[bf(("pb", 0)), bf("modrow")], outs=[bf("modrow")])
    store(mod_d.ap(), modrow, ins=[bf("modrow")], outs=[bf("mod_d")])
    load(modc[:, :], mod_d.ap().rearrange("o (v k p) -> p o v k", v=6, p=128),
         outs=[bf("modc")], ins=[bf("mod_d")], slow=True)
    for v in (1, 4):
        V.emit(lambda e, v=v: e.tensor_scalar_add(modc[:, v * KD:(v + 1) * KD],
                                                  modc[:, v * KD:(v + 1) * KD], 1.0),
               ins=[bf("modc")], outs=[bf("modc")])
    gtmp = fsl(17200, KD)
    gtmp2 = fsl(17300, KD)
    for v in (2, 5):
        gsl = modc[:, v * KD:(v + 1) * KD]
        S.emit(lambda e, g=gsl: e.activation(gtmp, g, AF.Sign),
               ins=[bf("modc")], outs=[bf("gtmp")])
        S.emit(lambda e: e.activation(gtmp2, gtmp, AF.Abs),
               ins=[bf("gtmp")], outs=[bf("gtmp2")])
        V.emit(lambda e: e.tensor_scalar(gtmp2, gtmp2, -1.0, 1.0, OP.mult, OP.add),
               ins=[bf("gtmp2")], outs=[bf("gtmp2")])
        V.emit(lambda e: e.tensor_tensor(gtmp, gtmp, gtmp2, OP.add),
               ins=[bf("gtmp"), bf("gtmp2")], outs=[bf("gtmp")])
        S.emit(lambda e, g=gsl: e.activation(gtmp2, g, AF.Abs),
               ins=[bf("modc")], outs=[bf("gtmp2")])
        V.emit(lambda e: e.tensor_single_scalar(gtmp2, gtmp2, 0.25, OP.max),
               ins=[bf("gtmp2")], outs=[bf("gtmp2")])
        V.emit(lambda e, g=gsl: e.tensor_tensor(g, gtmp, gtmp2, OP.mult),
               ins=[bf("gtmp"), bf("gtmp2")], outs=[bf("modc")])

    barrier()
    if upto == "setup":
        es.close()
        return nc

    def p_phase():
        xtm = [fsl(0, JT * D), fsl(JT * D, JT * D)]
        xfm_st = fsl(2 * JT * D, KD * T)
        sqt = fsl(2 * JT * D + KD * T, T)
        for tc in range(NT):
            sl = tc % 2
            load(xtm[sl], x_in.ap()[tc * T:(tc + 1) * T, :]
                 .rearrange("(j p) f -> p j f", p=128), outs=[bf(("xtm", sl))])
            for k in range(KD):
                bk = k % 4
                trchain(("pb", bk),
                        [(pbank[bk][:, j * 128:(j + 1) * 128],
                          xtm[sl][:, j * D + k * 128:j * D + k * 128 + 128])
                         for j in range(JT)],
                        [bf(("xtm", sl))])
                xc = xfm_st[:, k * T:(k + 1) * T]
                V.emit(lambda e, xc=xc, bk=bk: e.tensor_copy(xc, pbank[bk][:, :T]),
                       ins=[bf(("pb", bk))], outs=[bf(("xfc", k))])
                V.emit(lambda e, xc=xc: e.tensor_tensor(sqt, xc, xc, OP.mult),
                       ins=[bf(("xfc", k))], outs=[bf("sqt")])
                mmchain(("pb", 4), [(pbank[4][0:1, :T], ones_col[:, :], xc,
                                     k == 0, k == KD - 1)],
                        [bf("ones_col"), bf(("xfc", k))])
                mmchain(("pb", 5), [(pbank[5][0:1, :T], ones_col[:, :], sqt,
                                     k == 0, k == KD - 1)],
                        [bf("ones_col"), bf("sqt")])
            store(x_fm_d.ap()[:, :, tc * T:(tc + 1) * T].rearrange("c p t -> p c t"),
                  xfm_st, ins=[bf(("xfc", k)) for k in range(KD)],
                  outs=[bf(("xfm", tc))])
            S.emit(lambda e: e.activation(row_st[:, :], pbank[4][0:1, :T], AF.Copy),
                   ins=[bf(("pb", 4))], outs=[bf("row_st")])
            S.emit(lambda e: e.activation(row_st2[:, :], pbank[5][0:1, :T], AF.Copy),
                   ins=[bf(("pb", 5))], outs=[bf("row_st2")])
            store(rows_d.ap()[0:1, tc * T:(tc + 1) * T], row_st[:, :],
                  ins=[bf("row_st")], outs=[bf(("rowst", 0, tc))])
            store(rows_d.ap()[1:2, tc * T:(tc + 1) * T], row_st2[:, :],
                  ins=[bf("row_st2")], outs=[bf(("rowst", 1, tc))])

    def stats_finalize(src0, dst_rs, in_bufs):
        m_ = s64a[:, 0:LP]
        q_ = s64a[:, LP:2 * LP]
        v_ = s64a[:, 2 * LP:3 * LP]
        t_ = s64a[:, 3 * LP:4 * LP]
        load(m_, rows_d.ap()[src0:src0 + 1, :].rearrange("o (p j) -> p o j", p=128),
             outs=[bf("s64m")], ins=in_bufs)
        load(q_, rows_d.ap()[src0 + 1:src0 + 2, :]
             .rearrange("o (p j) -> p o j", p=128),
             outs=[bf("s64q")], ins=in_bufs)
        V.emit(lambda e: e.tensor_scalar_mul(m_, m_, 1.0 / D), ins=[bf("s64m")],
               outs=[bf("s64m")])
        V.emit(lambda e: e.tensor_scalar_mul(q_, q_, 1.0 / D), ins=[bf("s64q")],
               outs=[bf("s64q")])
        V.emit(lambda e: e.tensor_tensor(t_, m_, m_, OP.mult), ins=[bf("s64m")],
               outs=[bf("s64t")])
        V.emit(lambda e: e.tensor_tensor(v_, q_, t_, OP.subtract),
               ins=[bf("s64q"), bf("s64t")], outs=[bf("s64v")])
        S.emit(lambda e: e.activation(v_, v_, AF.Ln, bias=EPS), ins=[bf("s64v")],
               outs=[bf("s64v")])
        S.emit(lambda e: e.activation(v_, v_, AF.Exp, scale=-0.5), ins=[bf("s64v")],
               outs=[bf("s64v")])
        V.emit(lambda e: e.tensor_tensor(t_, m_, v_, OP.mult),
               ins=[bf("s64m"), bf("s64v")], outs=[bf("s64t")])
        store(rows_d.ap()[dst_rs:dst_rs + 1, :]
              .rearrange("o (p j) -> p o j", p=128), v_,
              ins=[bf("s64v")], outs=[bf(("rowst", dst_rs))])
        store(rows_d.ap()[dst_rs + 1:dst_rs + 2, :]
              .rearrange("o (p j) -> p o j", p=128), t_,
              ins=[bf("s64t")], outs=[bf(("rowst", dst_rs + 1))])

    p_phase()
    stats_finalize(0, 2, [bf(("rowst", r, tc)) for r in (0, 1) for tc in range(NT)])
    barrier()
    if upto == "P":
        es.close()
        return nc

    def bcast_rows(row_rs, tc, bank_rs, bank_mrs):
        ib = [bf(("rowst", row_rs)), bf(("rowst", row_rs + 1))]
        load(stat_st[0:2, :], rows_d.ap()[row_rs:row_rs + 2, tc * T:(tc + 1) * T],
             outs=[bf("stat_st")], ins=ib)
        mmchain(("pb", bank_rs), [(pbank[bank_rs][:, :T], sel4[:, 0:128],
                                   stat_st[:, :], True, True)],
                [bf("sel4"), bf("stat_st")])
        mmchain(("pb", bank_mrs), [(pbank[bank_mrs][:, :T], sel4[:, 128:256],
                                    stat_st[:, :], True, True)],
                [bf("sel4"), bf("stat_st")])

    def ln_apply(x_ap, x_b, bank_rs, bank_mrs, t1_ap, t1_b, out_ap, out_b,
                 sc_col, bi_col, modb):
        V.emit(lambda e: e.scalar_tensor_tensor(
            t1_ap, x_ap, 1.0, pbank[bank_rs][:, :T], OP.bypass, OP.mult),
            ins=[x_b, bf(("pb", bank_rs))], outs=[t1_b])
        V.emit(lambda e: e.tensor_tensor(t1_ap, t1_ap, pbank[bank_mrs][:, :T],
                                         OP.subtract),
               ins=[t1_b, bf(("pb", bank_mrs))], outs=[t1_b])
        V.emit(lambda e: e.tensor_scalar(out_ap, t1_ap, sc_col, bi_col,
                                         OP.mult, OP.add),
               ins=[t1_b, modb], outs=[out_b])

    def ln_wide(x_ap, x_b, out_ap, out_b, bias_v, scale_v, mtab, mtab_b):
        rsb = pbank[6][:, :T].unsqueeze(1).broadcast_to([128, KD, T])
        mrsb = pbank[7][:, :T].unsqueeze(1).broadcast_to([128, KD, T])
        x3 = x_ap.rearrange("p (k t) -> p k t", k=KD)
        o3 = out_ap.rearrange("p (k t) -> p k t", k=KD)
        V.emit(lambda e: e.tensor_tensor(o3, x3, rsb, OP.mult),
               ins=[x_b, bf(("pb", 6))], outs=[out_b])
        V.emit(lambda e: e.tensor_tensor(o3, o3, mrsb, OP.subtract),
               ins=[out_b, bf(("pb", 7))], outs=[out_b])
        scv = mtab[:, scale_v:scale_v + KD].unsqueeze(2).broadcast_to([128, KD, T])
        biv = mtab[:, bias_v:bias_v + KD].unsqueeze(2).broadcast_to([128, KD, T])
        V.emit(lambda e: e.tensor_tensor(o3, o3, scv, OP.mult),
               ins=[out_b, mtab_b], outs=[out_b])
        V.emit(lambda e: e.tensor_tensor(o3, o3, biv, OP.add),
               ins=[out_b, mtab_b], outs=[out_b])

    def load_w_pair(w_dram, col0, ncol, hi_off, lo_off, tag, wstg, wstg_key):
        nk = w_dram.shape[0] // 128
        for k in range(nk):
            load(wstg[:, :ncol], w_dram.ap()[k * 128:(k + 1) * 128, col0:col0 + ncol],
                 outs=[bf(wstg_key)])
            split_pair(wstg[:, :ncol], bsl(hi_off + k * ncol, ncol),
                       bsl(lo_off + k * ncol, ncol),
                       bf(wstg_key), bf((tag, "h", k)), bf((tag, "l", k)))

    wA_h, wA_l = 0, KD * 1024
    hA_h, hA_l = 2 * KD * 1024, 2 * KD * 1024 + KD * T

    def a1_phase():
        xfmt = [fsl(0, KD * T), fsl(KD * T, KD * T)]
        t1t = [fsl(2 * KD * T, T), fsl(2 * KD * T + T, T)]
        CV = T + 3
        h1t = [fsl(2 * KD * T + 2 * T + 2 * CV + 12 * T, T),
               fsl(2 * KD * T + 2 * T + 2 * CV + 13 * T, T)]
        xcv = [fsl(2 * KD * T + 2 * T, CV), fsl(2 * KD * T + 2 * T + CV, CV)]
        ut = fsl(2 * KD * T + 2 * T + 2 * CV, 8 * T)
        sgt = [fsl(2 * KD * T + 2 * T + 2 * CV + 8 * T, T),
               fsl(2 * KD * T + 2 * T + 2 * CV + 9 * T, T)]
        wstg = fsl(2 * KD * T + 2 * T + 2 * CV + 10 * T, 1024)
        for og in range(2):
            load_w_pair(w_in_T, og * 1024, 1024, wA_h, wA_l, "wA", wstg, "wstgA")
            for tc in range(NT):
                sl = tc % 2
                if og == 0:
                    load(xfmt[sl], x_fm_d.ap()[:, :, tc * T:(tc + 1) * T]
                         .rearrange("c p t -> p c t"),
                         outs=[bf(("xfmt", sl))], ins=[bf(("xfm", tc))])
                    bcast_rows(2, tc, 6, 7)
                    for k in range(KD):
                        q = k % 2
                        ln_apply(xfmt[sl][:, k * T:(k + 1) * T], bf(("xfmt", sl)),
                                 6, 7, t1t[q], bf(("t1t", q)), h1t[q],
                                 bf(("h1t", q)),
                                 modc[:, KD + k:KD + k + 1],
                                 modc[:, k:k + 1], bf("modc"))
                        split_pair(h1t[q], bsl(hA_h + k * T, T),
                                   bsl(hA_l + k * T, T),
                                   bf(("h1t", q)), bf(("hh", k)), bf(("hl", k)))
                    store(h_hi_d.ap()[:, :, tc * T:(tc + 1) * T]
                          .rearrange("c p t -> p c t"), bsl(hA_h, KD * T),
                          ins=[bf(("hh", k)) for k in range(KD)],
                          outs=[bf(("hhd", tc))])
                    store(h_lo_d.ap()[:, :, tc * T:(tc + 1) * T]
                          .rearrange("c p t -> p c t"), bsl(hA_l, KD * T),
                          ins=[bf(("hl", k)) for k in range(KD)],
                          outs=[bf(("hld", tc))])
                else:
                    load(bsl(hA_h, KD * T), h_hi_d.ap()[:, :, tc * T:(tc + 1) * T]
                         .rearrange("c p t -> p c t"),
                         outs=[bf(("hh", k)) for k in range(KD)],
                         ins=[bf(("hhd", tc))])
                    load(bsl(hA_l, KD * T), h_lo_d.ap()[:, :, tc * T:(tc + 1) * T]
                         .rearrange("c p t -> p c t"),
                         outs=[bf(("hl", k)) for k in range(KD)],
                         ins=[bf(("hld", tc))])
                for c in range(8):
                    cc = og * 8 + c
                    bk = c % 2
                    mms = []
                    dep = []
                    for k in range(KD):
                        lh = bsl(wA_h + k * 1024 + c * 128, 128)
                        ll = bsl(wA_l + k * 1024 + c * 128, 128)
                        rh = bsl(hA_h + k * T, T)
                        rl = bsl(hA_l + k * T, T)
                        mms += [(pbank[bk][:, :T], lh, rh, k == 0, False),
                                (pbank[bk][:, :T], lh, rl, False, False),
                                (pbank[bk][:, :T], ll, rh, False, k == KD - 1)]
                        dep += [bf(("wA", "h", k)), bf(("wA", "l", k)),
                                bf(("hh", k)), bf(("hl", k))]
                    mmchain(("pb", bk), mms, dep)
                    xv = xcv[c % 2]
                    V.emit(lambda e, xv=xv, cc=cc: e.tensor_copy(
                        xv[:, 0:3], halo[:, cc * 4:cc * 4 + 3]),
                        ins=[bf(("halo", cc))], outs=[bf(("xcv", c % 2))])
                    V.emit(lambda e, xv=xv, bk=bk: e.tensor_copy(
                        xv[:, 3:3 + T], pbank[bk][:, :T]),
                        ins=[bf(("pb", bk))], outs=[bf(("xcv", c % 2))])
                    V.emit(lambda e, xv=xv, cc=cc: e.tensor_copy(
                        halo[:, cc * 4:cc * 4 + 3], xv[:, T:T + 3]),
                        ins=[bf(("xcv", c % 2))], outs=[bf(("halo", cc))])
                    us = ut[:, c * T:(c + 1) * T]
                    sg_ = sgt[c % 2]
                    V.emit(lambda e, xv=xv, us=us, cc=cc: e.tensor_scalar(
                        us, xv[:, 0:T], conv_w[:, cc * 4:cc * 4 + 1], None, OP.mult),
                        ins=[bf(("xcv", c % 2)), bf("conv_w")], outs=[bf(("ut", c))])
                    for w in range(1, 4):
                        V.emit(lambda e, xv=xv, us=us, cc=cc, w=w:
                               e.scalar_tensor_tensor(
                                   us, xv[:, w:w + T],
                                   conv_w[:, cc * 4 + w:cc * 4 + w + 1],
                                   us, OP.mult, OP.add),
                               ins=[bf(("xcv", c % 2)), bf("conv_w"), bf(("ut", c))],
                               outs=[bf(("ut", c))])
                    V.emit(lambda e, us=us, cc=cc: e.tensor_scalar(
                        us, us, conv_b[:, cc:cc + 1], None, OP.add),
                        ins=[bf(("ut", c)), bf("conv_b")], outs=[bf(("ut", c))])
                    S.emit(lambda e, us=us, sg_=sg_: e.activation(
                        sg_, us, AF.Sigmoid),
                        ins=[bf(("ut", c))], outs=[bf(("sgt", c % 2))])
                    G.emit(lambda e, us=us, sg_=sg_: e.tensor_tensor(
                        us, us, sg_, OP.mult),
                        ins=[bf(("ut", c)), bf(("sgt", c % 2))], outs=[bf(("ut", c))])
                store(u_d.ap()[og * 8:(og + 1) * 8, :, tc * T:(tc + 1) * T]
                      .rearrange("c p t -> p c t"), ut,
                      ins=[bf(("ut", c)) for c in range(8)],
                      outs=[bf(("ud", og * 8 + c, tc)) for c in range(8)])
            barrier()

    a1_phase()
    if upto == "A1":
        es.close()
        return nc

    def a2_phase():
        zstg = fsl(0, 8 * T)
        ustg = fsl(8 * T, 8 * T)
        xdst = fsl(16 * T, T, p=XD)
        wstg = fsl(17 * T, 1024)
        uph_o = 2 * KD * 1024 + 2 * KD * T
        upl_o = uph_o + 8 * T
        for og in range(2):
            load_w_pair(w_in_T, DI + og * 1024, 1024, wA_h, wA_l, "wZ", wstg, "wstgZ")
            for tc in range(NT):
                load(bsl(hA_h, KD * T), h_hi_d.ap()[:, :, tc * T:(tc + 1) * T]
                     .rearrange("c p t -> p c t"),
                     outs=[bf(("hh", k)) for k in range(KD)], ins=[bf(("hhd", tc))])
                load(bsl(hA_l, KD * T), h_lo_d.ap()[:, :, tc * T:(tc + 1) * T]
                     .rearrange("c p t -> p c t"),
                     outs=[bf(("hl", k)) for k in range(KD)], ins=[bf(("hld", tc))])
                for c in range(8):
                    cc = og * 8 + c
                    bk = c % 2
                    mms = []
                    dep = []
                    for k in range(KD):
                        lh = bsl(wA_h + k * 1024 + c * 128, 128)
                        ll = bsl(wA_l + k * 1024 + c * 128, 128)
                        rh = bsl(hA_h + k * T, T)
                        rl = bsl(hA_l + k * T, T)
                        mms += [(pbank[bk][:, :T], lh, rh, k == 0, False),
                                (pbank[bk][:, :T], lh, rl, False, False),
                                (pbank[bk][:, :T], ll, rh, False, k == KD - 1)]
                        dep += [bf(("wZ", "h", k)), bf(("wZ", "l", k)),
                                bf(("hh", k)), bf(("hl", k))]
                    mmchain(("pb", bk), mms, dep)
                    V.emit(lambda e, c=c, bk=bk: e.tensor_copy(
                        zstg[:, c * T:(c + 1) * T], pbank[bk][:, :T]),
                        ins=[bf(("pb", bk))], outs=[bf(("zst", c))])
                store(z_d.ap()[og * 8:(og + 1) * 8, :, tc * T:(tc + 1) * T]
                      .rearrange("c p t -> p c t"), zstg,
                      ins=[bf(("zst", c)) for c in range(8)],
                      outs=[bf(("zd", og * 8 + c, tc)) for c in range(8)])
            barrier()
        for tc in range(NT):
            for half in range(2):
                load(ustg, u_d.ap()[half * 8:(half + 1) * 8, :, tc * T:(tc + 1) * T]
                     .rearrange("c p t -> p c t"),
                     outs=[bf("ustg")],
                     ins=[bf(("ud", half * 8 + c, tc)) for c in range(8)])
                for k in range(8):
                    split_pair(ustg[:, k * T:(k + 1) * T], bsl(uph_o + k * T, T),
                               bsl(upl_o + k * T, T), bf("ustg"),
                               bf(("uph", k)), bf(("upl", k)))
                mms = []
                dep = [bf("xpj_h"), bf("xpj_l")]
                for k in range(8):
                    kk = half * 8 + k
                    lh = xpj_h[:, kk * XD:(kk + 1) * XD]
                    ll = xpj_l[:, kk * XD:(kk + 1) * XD]
                    rh = bsl(uph_o + k * T, T)
                    rl = bsl(upl_o + k * T, T)
                    mms += [(pbank[4][:XD, :T], lh, rh, half == 0 and k == 0, False),
                            (pbank[4][:XD, :T], lh, rl, False, False),
                            (pbank[4][:XD, :T], ll, rh, False,
                             half == 1 and k == 7)]
                    dep += [bf(("uph", k)), bf(("upl", k))]
                mmchain(("pb", 4), mms, dep)
            V.emit(lambda e: e.tensor_copy(xdst, pbank[4][:XD, :T]),
                   ins=[bf(("pb", 4))], outs=[bf("xdst")])
            store(xdbl_d.ap()[:, tc * T:(tc + 1) * T], xdst, ins=[bf("xdst")],
                  outs=[bf(("xdbl", tc))])
        barrier()

    a2_phase()
    if upto == "A2":
        es.close()
        return nc

    def sc_phase():
        NB = 4
        WB = NB * T
        sel32 = bsl(2 * T, 32 * 128, p=32)
        load(sel32, sel32_in.ap(), outs=[bf("sel32")])
        dstp = fsl(0, WB)
        wstp = fsl(WB, WB)
        ystp = fsl(2 * WB, WB)
        estp = fsl(2 * WB, WB)
        ustp = fsl(3 * WB, WB)
        dAf = fsl(4 * WB, WB)
        bbf = [fsl(5 * WB, WB), fsl(6 * WB, WB)]
        hff = [fsl(7 * WB, WB), fsl(8 * WB, WB)]
        hlast = fsl(9 * WB, KDI * NS)
        cev = [fsl(9 * WB + KDI * NS + T, T), fsl(9 * WB + KDI * NS + 2 * T, T)]
        dtl_f = fsl(9 * WB + KDI * NS, T, p=RK)
        dtl_h = bsl(0, T, p=RK)
        dtl_l = bsl(T, T, p=RK)
        hl_tok = V.emit(lambda e: e.memset(hlast, 0.0), outs=[bf("hlast")])
        for _c in range(KDI):
            for _n in range(NS):
                bf(("hl8", _c, _n)).w = hl_tok
        for tc in range(NT):
            load(dtl_f, xdbl_d.ap()[0:RK, tc * T:(tc + 1) * T],
                 outs=[bf("dtl_f")], ins=[bf(("xdbl", tc))])
            split_pair(dtl_f, dtl_h, dtl_l, bf("dtl_f"), bf("dtl_h"), bf("dtl_l"))
            load(bc_f[:, :], xdbl_d.ap()[RK:RK + 32, tc * T:(tc + 1) * T],
                 outs=[bf("bc_f")], ins=[bf(("xdbl", tc))])
            split_pair(bc_f[:, :], bc_h[:, :], bc_l[:, :], bf("bc_f"),
                       bf("bc_h"), bf("bc_l"))
            for cb in range(KDI // NB):
                c0 = cb * NB
                for c4 in range(NB):
                    c = c0 + c4
                    bk = c4 % 2
                    dck = dstp[:, c4 * T:(c4 + 1) * T]
                    mmchain(("pb", bk),
                            [(pbank[bk][:, :T], dtw_h[:, c * 128:(c + 1) * 128],
                              dtl_h, True, False),
                             (pbank[bk][:, :T], dtw_h[:, c * 128:(c + 1) * 128],
                              dtl_l, False, False),
                             (pbank[bk][:, :T], dtw_l[:, c * 128:(c + 1) * 128],
                              dtl_h, False, True)],
                            [bf("dtw_h"), bf("dtw_l"), bf("dtl_h"), bf("dtl_l")])
                    V.emit(lambda e, dck=dck, bk=bk, c=c: e.tensor_scalar(
                        dck, pbank[bk][:, :T], dt_b[:, c:c + 1], None, OP.add),
                        ins=[bf(("pb", bk)), bf("dt_b")], outs=[bf("dstp")])
                S.emit(lambda e: e.activation(estp, dstp, AF.Abs),
                       ins=[bf("dstp")], outs=[bf("estp")])
                S.emit(lambda e: e.activation(estp, estp, AF.Exp, scale=-1.0),
                       ins=[bf("estp")], outs=[bf("estp")])
                S.emit(lambda e: e.activation(estp, estp, AF.Ln, bias=1.0),
                       ins=[bf("estp")], outs=[bf("estp")])
                V.emit(lambda e: e.scalar_tensor_tensor(
                    dstp, dstp, 0.0, estp, OP.max, OP.add),
                    ins=[bf("dstp"), bf("estp")], outs=[bf("dstp")])
                load(ustp, u_d.ap()[c0:c0 + NB, :, tc * T:(tc + 1) * T]
                     .rearrange("c p t -> p c t"),
                     outs=[bf("ustp")], ins=[bf(("ud", c0 + i, tc)) for i in range(NB)])
                G.emit(lambda e: e.tensor_tensor(wstp, dstp, ustp, OP.mult),
                       ins=[bf("dstp"), bf("ustp")], outs=[bf("wstp")])
                dAb = [bf("dAf"), bf("ustp")]
                for n in range(NS):
                    n2 = n % 2
                    bkB = 2 + n2
                    bkC = 4 + n2
                    mmchain(("pb", bkB),
                            [(pbank[bkB][:, :T], sel32[:, n * 128:(n + 1) * 128],
                              bc_h[:, :], True, False),
                             (pbank[bkB][:, :T], sel32[:, n * 128:(n + 1) * 128],
                              bc_l[:, :], False, True)],
                            [bf("sel32"), bf("bc_h"), bf("bc_l")])
                    mmchain(("pb", bkC),
                            [(pbank[bkC][:, :T],
                              sel32[:, (16 + n) * 128:(17 + n) * 128],
                              bc_h[:, :], True, False),
                             (pbank[bkC][:, :T],
                              sel32[:, (16 + n) * 128:(17 + n) * 128],
                              bc_l[:, :], False, True)],
                            [bf("sel32"), bf("bc_h"), bf("bc_l")])
                    bbv = pbank[bkB][:, :T].unsqueeze(1).broadcast_to([128, NB, T])
                    ccv = pbank[bkC][:, :T].unsqueeze(1).broadcast_to([128, NB, T])
                    bb_ = bbf[n2]
                    hf_ = hff[n2]
                    dA_ = dAf if n2 == 0 else ustp
                    for c4 in range(NB):
                        c = c0 + c4
                        S.emit(lambda e, dA_=dA_, c4=c4, c=c, n=n: e.activation(
                            dA_[:, c4 * T:(c4 + 1) * T],
                            dstp[:, c4 * T:(c4 + 1) * T], AF.Exp,
                            scale=a_neg[:, c * NS + n:c * NS + n + 1]),
                            ins=[bf("dstp"), bf("a_neg")], outs=[dAb[n2]])
                    V.emit(lambda e, bb_=bb_, bbv=bbv: e.tensor_tensor(
                        bb_.rearrange("p (c t) -> p c t", c=NB),
                        wstp.rearrange("p (c t) -> p c t", c=NB), bbv, OP.mult),
                        ins=[bf("wstp"), bf(("pb", bkB))], outs=[bf(("bbf", n2))])
                    for c4 in range(NB):
                        c = c0 + c4
                        V.emit(lambda e, hf_=hf_, bb_=bb_, dA_=dA_, c4=c4, c=c, n=n:
                               e.tensor_tensor_scan(
                                   hf_[:, c4 * T:(c4 + 1) * T],
                                   dA_[:, c4 * T:(c4 + 1) * T],
                                   bb_[:, c4 * T:(c4 + 1) * T],
                                   hlast[:, c * NS + n:c * NS + n + 1],
                                   OP.mult, OP.add),
                               ins=[dAb[n2], bf(("bbf", n2)), bf(("hl8", c, n))],
                               outs=[bf(("hff", n2))])
                        S.emit(lambda e, hf_=hf_, c4=c4, c=c, n=n: e.activation(
                            hlast[:, c * NS + n:c * NS + n + 1],
                            hf_[:, c4 * T + T - 1:c4 * T + T], AF.Copy),
                            ins=[bf(("hff", n2))], outs=[bf(("hl8", c, n))])
                    V.emit(lambda e, hf_=hf_, bb_=bb_, ccv=ccv: e.tensor_tensor(
                        bb_.rearrange("p (c t) -> p c t", c=NB),
                        hf_.rearrange("p (c t) -> p c t", c=NB), ccv, OP.mult),
                        ins=[bf(("hff", n2)), bf(("pb", bkC))], outs=[bf(("bbf", n2))])
                    if n == 0:
                        G.emit(lambda e, bb_=bb_: e.tensor_copy(ystp, bb_),
                               ins=[bf(("bbf", n2))], outs=[bf("estp")])
                    else:
                        G.emit(lambda e, bb_=bb_: e.tensor_tensor(
                            ystp, ystp, bb_, OP.add),
                            ins=[bf(("bbf", n2)), bf("estp")], outs=[bf("estp")])
                load(dAf, u_d.ap()[c0:c0 + NB, :, tc * T:(tc + 1) * T]
                     .rearrange("c p t -> p c t"),
                     outs=[bf("dAf")], ins=[bf(("ud", c0 + i, tc)) for i in range(NB)])
                for c4 in range(NB):
                    c = c0 + c4
                    yck = ystp[:, c4 * T:(c4 + 1) * T]
                    V.emit(lambda e, yck=yck, c=c, c4=c4: e.scalar_tensor_tensor(
                        yck, dAf[:, c4 * T:(c4 + 1) * T], d_par[:, c:c + 1],
                        yck, OP.mult, OP.add),
                        ins=[bf("dAf"), bf("d_par"), bf("estp")], outs=[bf("estp")])
                store(y_d.ap()[c0:c0 + NB, :, tc * T:(tc + 1) * T]
                      .rearrange("c p t -> p c t"), ystp,
                      ins=[bf("estp")],
                      outs=[bf(("yd", c0 + i, tc)) for i in range(NB)])
        barrier()

    sc_phase()
    if upto == "SC":
        es.close()
        return nc

    def d_gating():
        W4 = 4 * T
        yzt = [fsl(0, W4), fsl(W4, W4)]
        zt = [fsl(2 * W4, W4), fsl(3 * W4, W4)]
        sgt2 = [fsl(4 * W4, W4), fsl(5 * W4, W4)]
        ygh = [bsl(0, W4), bsl(W4, W4)]
        ygl = [bsl(2 * W4, W4), bsl(3 * W4, W4)]
        for tc in range(NT):
            for cg in range(KDI // 4):
                p2 = cg % 2
                load(yzt[p2], y_d.ap()[cg * 4:(cg + 1) * 4, :,
                                       tc * T:(tc + 1) * T]
                     .rearrange("c p t -> p c t"),
                     outs=[bf(("yzt", p2))],
                     ins=[bf(("yd", cg * 4 + i, tc)) for i in range(4)])
                load(zt[p2], z_d.ap()[cg * 4:(cg + 1) * 4, :,
                                      tc * T:(tc + 1) * T]
                     .rearrange("c p t -> p c t"),
                     outs=[bf(("zt", p2))],
                     ins=[bf(("zd", cg * 4 + i, tc)) for i in range(4)])
                S.emit(lambda e, p2=p2: e.activation(sgt2[p2], zt[p2],
                                                     AF.Sigmoid),
                       ins=[bf(("zt", p2))], outs=[bf(("sgt2", p2))])
                V.emit(lambda e, p2=p2: e.tensor_tensor(zt[p2], zt[p2], sgt2[p2],
                                                        OP.mult),
                       ins=[bf(("zt", p2)), bf(("sgt2", p2))], outs=[bf(("zt", p2))])
                V.emit(lambda e, p2=p2: e.tensor_tensor(yzt[p2], yzt[p2], zt[p2],
                                                        OP.mult),
                       ins=[bf(("yzt", p2)), bf(("zt", p2))], outs=[bf(("yzt", p2))])
                split_pair(yzt[p2], ygh[p2], ygl[p2], bf(("yzt", p2)),
                           bf(("yghb", p2)), bf(("yglb", p2)), eng=G)
                store(ygh_d.ap()[cg * 4:(cg + 1) * 4, :, tc * T:(tc + 1) * T]
                      .rearrange("c p t -> p c t"), ygh[p2],
                      ins=[bf(("yghb", p2))],
                      outs=[bf(("ygh", cg * 4 + i, tc)) for i in range(4)])
                store(ygl_d.ap()[cg * 4:(cg + 1) * 4, :, tc * T:(tc + 1) * T]
                      .rearrange("c p t -> p c t"), ygl[p2],
                      ins=[bf(("yglb", p2))],
                      outs=[bf(("ygl", cg * 4 + i, tc)) for i in range(4)])
        barrier()

    def d_proj():
        wO_h, wO_l = 0, KDI * 512
        ygh_o = 2 * KDI * 512
        ygl_o = ygh_o + KDI * T
        xft = fsl(0, 4 * T)
        x2t = fsl(4 * T, 4 * T)
        wstg = fsl(8 * T, 512)
        for og in range(2):
            load_w_pair(w_out_T, og * 512, 512, wO_h, wO_l, "wO", wstg, "wstgO")
            for tc in range(NT):
                load(bsl(ygh_o, KDI * T), ygh_d.ap()[:, :, tc * T:(tc + 1) * T]
                     .rearrange("c p t -> p c t"),
                     outs=[bf(("y2h", k)) for k in range(KDI)],
                     ins=[bf(("ygh", k, tc)) for k in range(KDI)])
                load(bsl(ygl_o, KDI * T), ygl_d.ap()[:, :, tc * T:(tc + 1) * T]
                     .rearrange("c p t -> p c t"),
                     outs=[bf(("y2l", k)) for k in range(KDI)],
                     ins=[bf(("ygl", k, tc)) for k in range(KDI)])
                load(xft, x_fm_d.ap()[og * 4:(og + 1) * 4, :, tc * T:(tc + 1) * T]
                     .rearrange("c p t -> p c t"),
                     outs=[bf("xft")], ins=[bf(("xfm", tc))])
                for o4 in range(4):
                    o = og * 4 + o4
                    bk = o4 % 2
                    mms = []
                    dep = []
                    for k in range(KDI):
                        lh = bsl(wO_h + k * 512 + o4 * 128, 128)
                        ll = bsl(wO_l + k * 512 + o4 * 128, 128)
                        rh = bsl(ygh_o + k * T, T)
                        rl = bsl(ygl_o + k * T, T)
                        mms += [(pbank[bk][:, :T], lh, rh, k == 0, False),
                                (pbank[bk][:, :T], lh, rl, False, False),
                                (pbank[bk][:, :T], ll, rh, False, k == KDI - 1)]
                        dep += [bf(("wO", "h", k)), bf(("wO", "l", k)),
                                bf(("y2h", k)), bf(("y2l", k))]
                    mmchain(("pb", bk), mms, dep)
                    x2c = x2t[:, o4 * T:(o4 + 1) * T]
                    V.emit(lambda e, x2c=x2c, o=o, o4=o4, bk=bk:
                           e.scalar_tensor_tensor(
                               x2c, pbank[bk][:, :T],
                               modc[:, 2 * KD + o:2 * KD + o + 1],
                               xft[:, o4 * T:(o4 + 1) * T], OP.mult, OP.add),
                           ins=[bf(("pb", bk)), bf("modc"), bf("xft")],
                           outs=[bf(("x2c", o4))])
                store(x2_d.ap()[og * 4:(og + 1) * 4, :, tc * T:(tc + 1) * T]
                      .rearrange("c p t -> p c t"), x2t,
                      ins=[bf(("x2c", o4)) for o4 in range(4)],
                      outs=[bf(("x2d", og, tc))])
            barrier()

    d_gating()
    d_proj()
    if upto == "D":
        es.close()
        return nc

    def cross_max_to_cc(col_idx, key, slot):
        store(colb_d.ap(), mcol[:, col_idx:col_idx + 1], ins=[bf("mcol")],
              outs=[bf("colb")])
        load(r128[:, :], colb_d.ap().rearrange("p o -> o p"),
             outs=[bf("r128")], ins=[bf("colb")])
        V.emit(lambda e: e.tensor_reduce(r1t[0:1, 0:1], r128[:, :], AX, OP.max),
               ins=[bf("r128")], outs=[bf("r1t")])
        V.emit(lambda e: e.tensor_scalar(r1t[0:1, 0:1], r1t[0:1, 0:1], 1.0 / QM,
                                         1e-12, OP.mult, OP.max),
               ins=[bf("r1t")], outs=[bf("r1t")])
        store(cc_i[key].ap()[0:1, slot:slot + 1], r1t[0:1, 0:1], ins=[bf("r1t")],
              outs=[bf(("cci", key, slot))])

    def cc_run(key, nslots):
        G.emit(lambda e: e.collective_compute(
            "AllReduce", OP.max, replica_groups=[core_ids],
            ins=[cc_i[key].ap()], outs=[cc_o[key].ap()]),
            ins=[bf(("cci", key, s)) for s in range(nslots)],
            outs=[bf(("cco", key))])

    def scalars_to_cols(key, pairs):
        w = cc_o[key].shape[1]
        load(row_st[0:1, 0:w], cc_o[key].ap(), outs=[bf("srow")],
             ins=[bf(("cco", key))])
        for (si, dcol, rcol) in pairs:
            mmchain(("pb", 7), [(pbank[7][:, 0:1], ones_row[:, :],
                                 row_st[0:1, si:si + 1], True, True)],
                    [bf("ones_row"), bf("srow")])
            V.emit(lambda e, dcol=dcol: e.tensor_copy(rcols[:, dcol:dcol + 1],
                                                      pbank[7][:, 0:1]),
                   ins=[bf(("pb", 7))], outs=[bf("rcols")])
            if rcol is not None:
                V.emit(lambda e, dcol=dcol, rcol=rcol: e.reciprocal(
                    rcols[:, rcol:rcol + 1], rcols[:, dcol:dcol + 1]),
                    ins=[bf("rcols")], outs=[bf("rcols")])

    def e0_stats():
        x2f = [fsl(0, KD * T), fsl(KD * T, KD * T)]
        sqt = fsl(2 * KD * T, KD * T)
        for tc in range(NT):
            sl = tc % 2
            load(x2f[sl], x2_d.ap()[:, :, tc * T:(tc + 1) * T]
                 .rearrange("c p t -> p c t"),
                 outs=[bf(("x2f", sl))],
                 ins=[bf(("x2d", 0, tc)), bf(("x2d", 1, tc))])
            V.emit(lambda e, sl=sl: e.tensor_tensor(sqt, x2f[sl], x2f[sl], OP.mult),
                   ins=[bf(("x2f", sl))], outs=[bf("sqt2")])
            mmchain(("pb", 4), [(pbank[4][0:1, :T], ones_col[:, :],
                                 x2f[sl][:, k * T:(k + 1) * T],
                                 k == 0, k == KD - 1) for k in range(KD)],
                    [bf("ones_col"), bf(("x2f", sl))])
            mmchain(("pb", 5), [(pbank[5][0:1, :T], ones_col[:, :],
                                 sqt[:, k * T:(k + 1) * T],
                                 k == 0, k == KD - 1) for k in range(KD)],
                    [bf("ones_col"), bf("sqt2")])
            S.emit(lambda e: e.activation(row_st[:, :], pbank[4][0:1, :T], AF.Copy),
                   ins=[bf(("pb", 4))], outs=[bf("row_st")])
            S.emit(lambda e: e.activation(row_st2[:, :], pbank[5][0:1, :T], AF.Copy),
                   ins=[bf(("pb", 5))], outs=[bf("row_st2")])
            store(rows_d.ap()[4:5, tc * T:(tc + 1) * T], row_st[:, :],
                  ins=[bf("row_st")], outs=[bf(("rowst", 4, tc))])
            store(rows_d.ap()[5:6, tc * T:(tc + 1) * T], row_st2[:, :],
                  ins=[bf("row_st2")], outs=[bf(("rowst", 5, tc))])
        stats_finalize(4, 6,
                       [bf(("rowst", r, tc)) for r in (4, 5) for tc in range(NT)])
        barrier()

    def e0_max():
        x2f = [fsl(0, KD * T), fsl(KD * T, KD * T)]
        h2 = fsl(2 * KD * T, KD * T)
        V.emit(lambda e: e.memset(mcol[:, 0:1], 0.0), ins=[bf("mcol")],
               outs=[bf("mcol")])
        for tc in range(NT):
            sl = tc % 2
            load(x2f[sl], x2_d.ap()[:, :, tc * T:(tc + 1) * T]
                 .rearrange("c p t -> p c t"),
                 outs=[bf(("x2f", sl))],
                 ins=[bf(("x2d", 0, tc)), bf(("x2d", 1, tc))])
            bcast_rows(6, tc, 6, 7)
            ln_wide(x2f[sl], bf(("x2f", sl)), h2, bf("h2e0"), 3 * KD, 4 * KD,
                    modc, bf("modc"))
            V.emit(lambda e: e.tensor_reduce(mcol[:, 7:8], h2, AX, OP.max,
                                             apply_absolute_value=True),
                   ins=[bf("h2e0")], outs=[bf("mcol")])
            V.emit(lambda e: e.tensor_tensor(mcol[:, 0:1], mcol[:, 0:1],
                                             mcol[:, 7:8], OP.max),
                   ins=[bf("mcol")], outs=[bf("mcol")])
        cross_max_to_cc(0, "1", 0)
        cc_run("1", 1)
        scalars_to_cols("1", [(0, 0, 1)])
        V.emit(lambda e: e.tensor_scalar(
            modc2[:, 0:KD], modc[:, 4 * KD:5 * KD], rcols[:, 1:2], None, OP.mult),
            ins=[bf("modc"), bf("rcols")], outs=[bf("modc2")])
        V.emit(lambda e: e.tensor_scalar(
            modc2[:, KD:2 * KD], modc[:, 3 * KD:4 * KD], rcols[:, 1:2], None,
            OP.mult),
            ins=[bf("modc"), bf("rcols")], outs=[bf("modc2")])
        barrier()

    e0_stats()
    e0_max()
    if upto == "E0":
        es.close()
        return nc

    def e1_quant():
        W8 = KD * T
        x2f = [fsl(0, W8), fsl(W8, W8)]
        h2s = fsl(2 * W8, W8)
        mth = fsl(3 * W8, W8)
        rth = fsl(4 * W8, W8)
        k_h = bsl(0, W8)
        k_l = bsl(W8, W8)
        for tc in range(NT):
            sl = tc % 2
            load(x2f[sl], x2_d.ap()[:, :, tc * T:(tc + 1) * T]
                 .rearrange("c p t -> p c t"),
                 outs=[bf(("x2f", sl))],
                 ins=[bf(("x2d", 0, tc)), bf(("x2d", 1, tc))])
            bcast_rows(6, tc, 6, 7)
            ln_wide(x2f[sl], bf(("x2f", sl)), h2s, bf("h2s"), KD, 0,
                    modc2, bf("modc2"))
            rne(h2s, bf("h2s"), rth, bf("rth"), mth, bf("mth"),
                x2f[sl], bf(("x2f", sl)))
            split256(rth, bf("rth"), k_h, bf("khh"), k_l, bf("kll"),
                     mth, bf("mth"))
            store(k1h_d.ap()[:, :, tc * T:(tc + 1) * T].rearrange("c p t -> p c t"),
                  k_h, ins=[bf("khh")], outs=[bf(("k1h", tc))])
            store(k1l_d.ap()[:, :, tc * T:(tc + 1) * T].rearrange("c p t -> p c t"),
                  k_l, ins=[bf("kll")], outs=[bf(("k1l", tc))])
        barrier()

    def lpbq_group(w_dram, tag, og, nko):
        I = w_dram.shape[1]
        NB = I // 32
        KI = I // 128
        wraw = fsl(0, I)
        wsc = fsl(I, NB)
        wm2 = fsl(I + NB, I)
        wdq = fsl(2 * I + NB, I)
        wtr = fsl(3 * I + NB, 128)
        for oi in range(nko):
            oc = og * nko + oi
            load(wraw, w_dram.ap()[oc * 128:(oc + 1) * 128, :], outs=[bf("wraw")])
            V.emit(lambda e: e.tensor_reduce(
                wsc, wraw.rearrange("p (b i) -> p b i", i=32), AX, OP.max,
                apply_absolute_value=True),
                ins=[bf("wraw")], outs=[bf("wsc")])
            V.emit(lambda e: e.tensor_scalar(wsc, wsc, 1.0 / 7.0, 1e-12,
                                             OP.mult, OP.max),
                   ins=[bf("wsc")], outs=[bf("wsc")])
            V.emit(lambda e, NB=NB: e.reciprocal(wm2[:, 0:NB], wsc),
                   ins=[bf("wsc")], outs=[bf("wmr")])
            V.emit(lambda e, NB=NB: e.tensor_tensor(
                wdq.rearrange("p (b i) -> p b i", i=32),
                wraw.rearrange("p (b i) -> p b i", i=32),
                wm2[:, 0:NB].unsqueeze(2).broadcast_to([128, NB, 32]), OP.mult),
                ins=[bf("wraw"), bf("wmr")], outs=[bf("wdq")])
            rne(wdq, bf("wdq"), wdq, bf("wdq"), wraw, bf("wraw"), wm2, bf("wm2"))
            V.emit(lambda e, NB=NB: e.tensor_tensor(
                wdq.rearrange("p (b i) -> p b i", i=32),
                wdq.rearrange("p (b i) -> p b i", i=32),
                wsc.unsqueeze(2).broadcast_to([128, NB, 32]), OP.mult),
                ins=[bf("wdq"), bf("wsc")], outs=[bf("wdq")])
            for k in range(KI):
                bk = k % 2
                trchain(("pb", bk), [(pbank[bk][:, :128],
                                      wdq[:, k * 128:(k + 1) * 128])], [bf("wdq")])
                V.emit(lambda e, bk=bk: e.tensor_copy(wtr, pbank[bk][:, :128]),
                       ins=[bf(("pb", bk))], outs=[bf("wtr")])
                split_pair(wtr, bsl(k * (nko * 128) + oi * 128, 128),
                           bsl(KI * nko * 128 + k * (nko * 128) + oi * 128, 128),
                           bf("wtr"), bf((tag, "h", k)), bf((tag, "l", k)))

    def ffn_mm(w_dram, tag, out_dram, out_key, mx_idx):
        kh_o = 2 * KD * 12 * 128
        kl_o = kh_o + KD * T
        ustg = fsl(4 * D + D // 32 + 256, 12 * T)
        V.emit(lambda e, mx_idx=mx_idx: e.memset(mcol[:, mx_idx:mx_idx + 1], 0.0),
               ins=[bf("mcol")], outs=[bf("mcol")])
        for og in range(2):
            lpbq_group(w_dram, tag, og, 12)
            for tc in range(NT):
                load(bsl(kh_o, KD * T), k1h_d.ap()[:, :, tc * T:(tc + 1) * T]
                     .rearrange("c p t -> p c t"),
                     outs=[bf(("k2h", k)) for k in range(KD)], ins=[bf(("k1h", tc))])
                load(bsl(kl_o, KD * T), k1l_d.ap()[:, :, tc * T:(tc + 1) * T]
                     .rearrange("c p t -> p c t"),
                     outs=[bf(("k2l", k)) for k in range(KD)], ins=[bf(("k1l", tc))])
                for oi in range(12):
                    bk = oi % 2
                    mms = []
                    dep = []
                    for k in range(KD):
                        lh = bsl(k * 1536 + oi * 128, 128)
                        ll = bsl(KD * 1536 + k * 1536 + oi * 128, 128)
                        rh = bsl(kh_o + k * T, T)
                        rl = bsl(kl_o + k * T, T)
                        mms += [(pbank[bk][:, :T], lh, rh, k == 0, False),
                                (pbank[bk][:, :T], lh, rl, False, False),
                                (pbank[bk][:, :T], ll, rh, False, k == KD - 1)]
                        dep += [bf((tag, "h", k)), bf((tag, "l", k)),
                                bf(("k2h", k)), bf(("k2l", k))]
                    mmchain(("pb", bk), mms, dep)
                    usl = ustg[:, oi * T:(oi + 1) * T]
                    S.emit(lambda e, usl=usl, bk=bk: e.activation(
                        usl, pbank[bk][:, :T], AF.Identity, bias=0.0,
                        scale=rcols[:, 0:1]),
                        ins=[bf(("pb", bk)), bf("rcols")], outs=[bf(("ust", oi))])
                V.emit(lambda e: e.tensor_reduce(
                    mcol[:, 7:8], ustg, AX, OP.max, apply_absolute_value=True),
                    ins=[bf(("ust", oi)) for oi in range(12)], outs=[bf("mcol")])
                V.emit(lambda e, mx_idx=mx_idx: e.tensor_tensor(
                    mcol[:, mx_idx:mx_idx + 1], mcol[:, mx_idx:mx_idx + 1],
                    mcol[:, 7:8], OP.max),
                    ins=[bf("mcol")], outs=[bf("mcol")])
                store(out_dram.ap()[og * 12:(og + 1) * 12, :, tc * T:(tc + 1) * T]
                      .rearrange("c p t -> p c t"), ustg,
                      ins=[bf(("ust", oi)) for oi in range(12)],
                      outs=[bf((out_key, og * 12 + oi, tc)) for oi in range(12)])
            barrier()

    e1_quant()
    if upto == "E1q":
        es.close()
        return nc
    ffn_mm(w_up_in, "wU", upraw_d, "upr", 2)
    if upto == "E1":
        es.close()
        return nc
    ffn_mm(w_gate_in, "wG", gateraw_d, "gtr", 3)
    cross_max_to_cc(2, "23", 0)
    cross_max_to_cc(3, "23", 1)
    cc_run("23", 2)
    scalars_to_cols("23", [(0, 2, 3), (1, 4, 5)])
    V.emit(lambda e: e.tensor_scalar_mul(rcols[:, 10:11], rcols[:, 4:5],
                                         1.0 / 65536.0),
           ins=[bf("rcols")], outs=[bf("rcols")])
    barrier()
    if upto == "E2":
        es.close()
        return nc

    def f_phase():
        W4 = 4 * T
        stg = {}
        for p in range(2):
            b0 = p * 5 * W4
            stg[p] = (fsl(b0, W4), fsl(b0 + W4, W4), fsl(b0 + 2 * W4, W4),
                      fsl(b0 + 3 * W4, W4), fsl(b0 + 4 * W4, W4))
        V.emit(lambda e: e.memset(mcol[:, 4:5], 0.0), ins=[bf("mcol")],
               outs=[bf("mcol")])
        V.emit(lambda e: e.memset(mcol[:, 1:2], 0.0), ins=[bf("mcol")],
               outs=[bf("mcol")])
        for tc in range(NT):
            for sg in range(6):
                it = tc * 6 + sg
                gp = False
                p = it % 2
                E = V
                acc = 1 if p else 4
                tmp = 6 if p else 7
                gstg, g1, mt, rg, ostg = stg[p]
                load(gstg, gateraw_d.ap()[sg * 4:(sg + 1) * 4, :,
                                          tc * T:(tc + 1) * T]
                     .rearrange("c p t -> p c t"),
                     outs=[bf(("gstg", p))],
                     ins=[bf(("gtr", sg * 4 + i, tc)) for i in range(4)])
                if not gp:
                    E.emit(lambda e, g1=g1, gstg=gstg: e.tensor_scalar(
                        g1, gstg, rcols[:, 5:6], None, OP.mult),
                        ins=[bf(("gstg", p)), bf("rcols")], outs=[bf(("g1f", p))])
                    rne_e(E, g1, bf(("g1f", p)), rg, bf(("rgf", p)),
                          mt, bf(("mtf", p)), gstg, bf(("gstg", p)))
                else:
                    bctt(g1, gstg, rcols[:, 5:6], OP.mult,
                         [bf(("gstg", p)), bf("rcols")], [bf(("g1f", p))], W4)
                    rne_g(g1, bf(("g1f", p)), rg, bf(("rgf", p)),
                          mt, bf(("mtf", p)), gstg, bf(("gstg", p)), W4)
                S.emit(lambda e, g1=g1, rg=rg: e.activation(
                    g1, rg, AF.Sigmoid, scale=rcols[:, 4:5]),
                    ins=[bf(("rgf", p)), bf("rcols")], outs=[bf(("g1f", p))])
                if not gp:
                    E.emit(lambda e, g1=g1: e.tensor_scalar_mul(g1, g1, 65536.0),
                           ins=[bf(("g1f", p))], outs=[bf(("g1f", p))])
                    rne_e(E, g1, bf(("g1f", p)), ostg, bf(("ostgF", p)),
                          mt, bf(("mtf", p)), gstg, bf(("gstg", p)))
                    E.emit(lambda e, ostg=ostg: e.tensor_single_scalar(
                        ostg, ostg, 65535.0, OP.min),
                        ins=[bf(("ostgF", p))], outs=[bf(("ostgF", p))])
                    V.emit(lambda e, ostg=ostg, rg=rg: e.scalar_tensor_tensor(
                        ostg, rg, rcols[:, 10:11], ostg, OP.mult, OP.mult),
                        ins=[bf(("rgf", p)), bf("rcols"), bf(("ostgF", p))],
                        outs=[bf(("ostgF", p))])
                else:
                    bctt(g1, g1, consts[:, 5:6], OP.mult,
                         [bf(("g1f", p)), bf("consts")], [bf(("g1f", p))], W4)
                    rne_g(g1, bf(("g1f", p)), ostg, bf(("ostgF", p)),
                          mt, bf(("mtf", p)), gstg, bf(("gstg", p)), W4)
                    V.emit(lambda e, ostg=ostg: e.tensor_single_scalar(
                        ostg, ostg, 65535.0, OP.min),
                        ins=[bf(("ostgF", p))], outs=[bf(("ostgF", p))])
                    bctt(mt, rg, rcols[:, 10:11], OP.mult,
                         [bf(("rgf", p)), bf("rcols")], [bf(("mtf", p))], W4)
                    gtt(ostg, mt, ostg, OP.mult,
                        [bf(("mtf", p)), bf(("ostgF", p))], [bf(("ostgF", p))])

                V.emit(lambda e, ostg=ostg, tmp=tmp: e.tensor_reduce(
                    mcol[:, tmp:tmp + 1], ostg, AX, OP.max,
                    apply_absolute_value=True),
                    ins=[bf(("ostgF", p))], outs=[bf("mcol")])
                V.emit(lambda e, acc=acc, tmp=tmp: e.tensor_tensor(
                    mcol[:, acc:acc + 1], mcol[:, acc:acc + 1],
                    mcol[:, tmp:tmp + 1], OP.max),
                    ins=[bf("mcol")], outs=[bf("mcol")])
                store(g2raw_d.ap()[sg * 4:(sg + 1) * 4, :, tc * T:(tc + 1) * T]
                      .rearrange("c p t -> p c t"), ostg,
                      ins=[bf(("ostgF", p))],
                      outs=[bf(("g2r", sg * 4 + i, tc)) for i in range(4)])
        V.emit(lambda e: e.tensor_tensor(mcol[:, 4:5], mcol[:, 4:5],
                                         mcol[:, 1:2], OP.max),
               ins=[bf("mcol")], outs=[bf("mcol")])
        cross_max_to_cc(4, "4", 0)
        cc_run("4", 1)
        scalars_to_cols("4", [(0, 6, 7)])
        barrier()

    f_phase()
    if upto == "F":
        es.close()
        return nc

    def g_phase():
        W4 = 4 * T
        stg = {}
        for p in range(2):
            b0 = p * 5 * W4
            stg[p] = (fsl(b0, W4), fsl(b0 + W4, W4), fsl(b0 + 2 * W4, W4),
                      fsl(b0 + 3 * W4, W4), fsl(b0 + 4 * W4, W4))
        V.emit(lambda e: e.tensor_tensor(rcols[:, 11:12], rcols[:, 2:3],
                                         rcols[:, 6:7], OP.mult),
               ins=[bf("rcols")], outs=[bf("rcols")])
        V.emit(lambda e: e.memset(mcol[:, 5:6], 0.0), ins=[bf("mcol")],
               outs=[bf("mcol")])
        V.emit(lambda e: e.memset(mcol[:, 1:2], 0.0), ins=[bf("mcol")],
               outs=[bf("mcol")])
        for tc in range(NT):
            for sg in range(6):
                it = tc * 6 + sg
                gp = False
                p = it % 2
                E = V
                acc = 1 if p else 5
                tmp = 6 if p else 7
                ustg, gstg, ostg, g1, mt = stg[p]
                load(ustg, upraw_d.ap()[sg * 4:(sg + 1) * 4, :,
                                        tc * T:(tc + 1) * T]
                     .rearrange("c p t -> p c t"),
                     outs=[bf(("ustgG", p))],
                     ins=[bf(("upr", sg * 4 + i, tc)) for i in range(4)])
                load(gstg, g2raw_d.ap()[sg * 4:(sg + 1) * 4, :,
                                        tc * T:(tc + 1) * T]
                     .rearrange("c p t -> p c t"),
                     outs=[bf(("gstgG", p))],
                     ins=[bf(("g2r", sg * 4 + i, tc)) for i in range(4)])
                if not gp:
                    E.emit(lambda e, g1=g1, ustg=ustg: e.tensor_scalar(
                        g1, ustg, rcols[:, 3:4], None, OP.mult),
                        ins=[bf(("ustgG", p)), bf("rcols")], outs=[bf(("g1g", p))])
                    rne_e(E, g1, bf(("g1g", p)), ustg, bf(("ustgG", p)),
                          mt, bf(("mtg", p)), ostg, bf(("ostgG", p)))
                    E.emit(lambda e, g1=g1, gstg=gstg: e.tensor_scalar(
                        g1, gstg, rcols[:, 7:8], None, OP.mult),
                        ins=[bf(("gstgG", p)), bf("rcols")], outs=[bf(("g1g", p))])
                    rne_e(E, g1, bf(("g1g", p)), gstg, bf(("gstgG", p)),
                          mt, bf(("mtg", p)), ostg, bf(("ostgG", p)))
                    V.emit(lambda e, ostg=ostg, ustg=ustg, gstg=gstg:
                           e.scalar_tensor_tensor(
                               ostg, ustg, rcols[:, 11:12], gstg,
                               OP.mult, OP.mult),
                           ins=[bf(("ustgG", p)), bf(("gstgG", p)), bf("rcols")],
                           outs=[bf(("ostgG", p))])
                else:
                    bctt(g1, ustg, rcols[:, 3:4], OP.mult,
                         [bf(("ustgG", p)), bf("rcols")], [bf(("g1g", p))], W4)
                    rne_g(g1, bf(("g1g", p)), ustg, bf(("ustgG", p)),
                          mt, bf(("mtg", p)), ostg, bf(("ostgG", p)), W4)
                    bctt(g1, gstg, rcols[:, 7:8], OP.mult,
                         [bf(("gstgG", p)), bf("rcols")], [bf(("g1g", p))], W4)
                    rne_g(g1, bf(("g1g", p)), gstg, bf(("gstgG", p)),
                          mt, bf(("mtg", p)), ostg, bf(("ostgG", p)), W4)
                    bctt(mt, ustg, rcols[:, 11:12], OP.mult,
                         [bf(("ustgG", p)), bf("rcols")], [bf(("mtg", p))], W4)
                    gtt(ostg, mt, gstg, OP.mult,
                        [bf(("mtg", p)), bf(("gstgG", p))], [bf(("ostgG", p))])

                V.emit(lambda e, ostg=ostg, tmp=tmp: e.tensor_reduce(
                    mcol[:, tmp:tmp + 1], ostg, AX, OP.max,
                    apply_absolute_value=True),
                    ins=[bf(("ostgG", p))], outs=[bf("mcol")])
                V.emit(lambda e, acc=acc, tmp=tmp: e.tensor_tensor(
                    mcol[:, acc:acc + 1], mcol[:, acc:acc + 1],
                    mcol[:, tmp:tmp + 1], OP.max),
                    ins=[bf("mcol")], outs=[bf("mcol")])
                store(oraw_d.ap()[sg * 4:(sg + 1) * 4, :, tc * T:(tc + 1) * T]
                      .rearrange("c p t -> p c t"), ostg,
                      ins=[bf(("ostgG", p))],
                      outs=[bf(("ord", sg * 4 + i, tc)) for i in range(4)])
        V.emit(lambda e: e.tensor_tensor(mcol[:, 5:6], mcol[:, 5:6],
                                         mcol[:, 1:2], OP.max),
               ins=[bf("mcol")], outs=[bf("mcol")])
        cross_max_to_cc(5, "5", 0)
        cc_run("5", 1)
        scalars_to_cols("5", [(0, 8, 9)])
        V.emit(lambda e: e.tensor_scalar(modc2[:, 2 * KD:3 * KD],
                                         modc[:, 5 * KD:6 * KD],
                                         rcols[:, 8:9], None, OP.mult),
               ins=[bf("modc"), bf("rcols")], outs=[bf("modc2")])
        barrier()

    g_phase()
    if upto == "G":
        es.close()
        return nc

    def h_quant():
        W4 = 4 * T
        stg = {}
        bstg = {}
        for p in range(2):
            b0 = p * 5 * W4
            stg[p] = (fsl(b0, W4), fsl(b0 + W4, W4), fsl(b0 + 2 * W4, W4),
                      fsl(b0 + 3 * W4, W4), fsl(b0 + 4 * W4, W4))
            bstg[p] = (bsl(p * 2 * W4, W4), bsl(p * 2 * W4 + W4, W4))
        for tc in range(NT):
            for sg in range(6):
                it = tc * 6 + sg
                gp = False
                p = it % 2
                E = V
                ostg, g1, mt, ro, t4 = stg[p]
                roh, rol = bstg[p]
                load(ostg, oraw_d.ap()[sg * 4:(sg + 1) * 4, :,
                                       tc * T:(tc + 1) * T]
                     .rearrange("c p t -> p c t"),
                     outs=[bf(("ostgH", p))],
                     ins=[bf(("ord", sg * 4 + i, tc)) for i in range(4)])
                if not gp:
                    E.emit(lambda e, g1=g1, ostg=ostg: e.tensor_scalar(
                        g1, ostg, rcols[:, 9:10], None, OP.mult),
                        ins=[bf(("ostgH", p)), bf("rcols")], outs=[bf(("g1h", p))])
                    rne_e(E, g1, bf(("g1h", p)), ro, bf(("roh2", p)),
                          mt, bf(("mth2", p)), ostg, bf(("ostgH", p)))
                    split256_e(E, ro, bf(("roh2", p)), roh, bf(("rohxx", p)),
                               rol, bf(("rolxx", p)), t4, bf(("t4h", p)))
                else:
                    bctt(g1, ostg, rcols[:, 9:10], OP.mult,
                         [bf(("ostgH", p)), bf("rcols")], [bf(("g1h", p))], W4)
                    rne_g(g1, bf(("g1h", p)), ro, bf(("roh2", p)),
                          mt, bf(("mth2", p)), ostg, bf(("ostgH", p)), W4)
                    split256_g(ro, bf(("roh2", p)), roh, bf(("rohxx", p)),
                               rol, bf(("rolxx", p)), t4, bf(("t4h", p)), W4)
                store(roh_d.ap()[sg * 4:(sg + 1) * 4, :, tc * T:(tc + 1) * T]
                      .rearrange("c p t -> p c t"), roh,
                      ins=[bf(("rohxx", p))],
                      outs=[bf(("rohd", sg * 4 + i, tc)) for i in range(4)])
                store(rol_d.ap()[sg * 4:(sg + 1) * 4, :, tc * T:(tc + 1) * T]
                      .rearrange("c p t -> p c t"), rol,
                      ins=[bf(("rolxx", p))],
                      outs=[bf(("rold", sg * 4 + i, tc)) for i in range(4)])
        barrier()

    def h_down():
        ro_b = 2 * KI3 * 256
        hb0 = 4 * I3 + I3 // 32 + 256
        x2sv = [fsl(hb0, 2 * T), fsl(hb0 + 2 * T, 2 * T)]
        finv = [fsl(hb0 + 4 * T, 2 * T), fsl(hb0 + 6 * T, 2 * T)]
        trsv = [fsl(hb0 + 8 * T, 2 * T), fsl(hb0 + 10 * T, 2 * T)]
        for wg in range(4):
            lpbq_group(w_down_in, "wD", wg, 2)
            for tc in range(NT):
                mb = 2 * (tc % 2)
                x2s = x2sv[tc % 2]
                fin = finv[tc % 2]
                trs = trsv[tc % 2]
                load(x2s, x2_d.ap()[wg * 2:(wg + 1) * 2, :, tc * T:(tc + 1) * T]
                     .rearrange("c p t -> p c t"),
                     outs=[bf(("x2s", tc % 2))], ins=[bf(("x2d", wg // 2, tc))])
                for q in range(4):
                    sl = q % 2
                    load(bsl(ro_b + sl * 12 * T, 6 * T),
                         roh_d.ap()[q * 6:(q + 1) * 6, :, tc * T:(tc + 1) * T]
                         .rearrange("c p t -> p c t"),
                         outs=[bf(("r4h", sl))],
                         ins=[bf(("rohd", q * 6 + k, tc)) for k in range(6)])
                    load(bsl(ro_b + sl * 12 * T + 6 * T, 6 * T),
                         rol_d.ap()[q * 6:(q + 1) * 6, :, tc * T:(tc + 1) * T]
                         .rearrange("c p t -> p c t"),
                         outs=[bf(("r4l", sl))],
                         ins=[bf(("rold", q * 6 + k, tc)) for k in range(6)])
                    for og1 in range(2):
                        mms = []
                        for k in range(6):
                            kk = q * 6 + k
                            lh = bsl(kk * 256 + og1 * 128, 128)
                            ll = bsl(KI3 * 256 + kk * 256 + og1 * 128, 128)
                            rh = bsl(ro_b + sl * 12 * T + k * T, T)
                            rl = bsl(ro_b + sl * 12 * T + 6 * T + k * T, T)
                            mms += [(pbank[mb + og1][:, :T], lh, rh,
                                     q == 0 and k == 0, False),
                                    (pbank[mb + og1][:, :T], lh, rl,
                                     False, False),
                                    (pbank[mb + og1][:, :T], ll, rh, False,
                                     q == 3 and k == 5)]
                        mmchain(("pb", mb + og1), mms,
                                [bf(("wD", "h", q * 6 + k)) for k in range(6)]
                                + [bf(("wD", "l", q * 6 + k)) for k in range(6)]
                                + [bf(("r4h", sl)), bf(("r4l", sl))])
                for og1 in range(2):
                    oc = wg * 2 + og1
                    fsl_o = fin[:, og1 * T:(og1 + 1) * T]
                    V.emit(lambda e, fsl_o=fsl_o, x2s=x2s, og1=og1, oc=oc, mb=mb:
                           e.scalar_tensor_tensor(
                        fsl_o, pbank[mb + og1][:, :T],
                        modc2[:, 2 * KD + oc:2 * KD + oc + 1],
                        x2s[:, og1 * T:(og1 + 1) * T], OP.mult, OP.add),
                        ins=[bf(("pb", mb + og1)), bf("modc2"),
                             bf(("x2s", tc % 2))],
                        outs=[bf(("finh", tc % 2, og1))])
                    tb = 4 + og1
                    trchain(("pb", tb),
                            [(pbank[tb][:, j * 128:(j + 1) * 128],
                              fsl_o[:, j * 128:(j + 1) * 128])
                             for j in range(JT)],
                            [bf(("finh", tc % 2, og1))])
                    tr_o = trs[:, og1 * T:(og1 + 1) * T]
                    V.emit(lambda e, tb=tb, tr_o=tr_o: e.tensor_copy(
                        tr_o, pbank[tb][:, 0:JT * 128]),
                        ins=[bf(("pb", tb))], outs=[bf(("trsb", tc % 2, og1))])
                    store(out_t.ap()[tc * T:(tc + 1) * T, oc * 128:(oc + 1) * 128]
                          .rearrange("(j p) f -> p j f", p=128),
                          tr_o.rearrange("p (j f) -> p j f", j=JT),
                          ins=[bf(("trsb", tc % 2, og1))],
                          outs=[bf(("outd", oc, tc))])
            barrier()

    h_quant()
    h_down()

    for _e in ENGS:
        _e.flush()
    fin_toks = [bf(("outd", og, tc)).w for og in range(8) for tc in range(NT)]
    best = {}
    for t in fin_toks:
        kk = id(t.sem)
        if kk not in best or best[kk].cnt < t.cnt:
            best[kk] = t

    def fin_body(eng):
        for t in best.values():
            eng.wait_ge(t.sem, t.cnt)

    blk.sync(fin_body)
    blk.gpsimd(fin_body)

    es.close()
    return nc


def _prep_inputs(inputs):
    import ml_dtypes
    asc = np.ascontiguousarray
    f32 = np.float32
    w = {}
    w["w_in_T"] = asc(np.asarray(inputs["L0_attn_in_proj_w"], dtype=f32).T)
    w["conv_w_in"] = asc(np.asarray(inputs["L0_attn_conv_w"], dtype=f32))
    w["conv_b_in"] = asc(np.asarray(inputs["L0_attn_conv_b"], dtype=f32))
    w["w_xproj_T"] = asc(np.asarray(inputs["L0_attn_x_proj_w"], dtype=f32).T)
    w["w_dt_T"] = asc(np.asarray(inputs["L0_attn_dt_proj_w"], dtype=f32).T)
    w["dt_b_in"] = asc(np.asarray(inputs["L0_attn_dt_proj_b"], dtype=f32))
    w["a_log_in"] = asc(np.asarray(inputs["L0_attn_A_log"], dtype=f32))
    w["d_par_in"] = asc(np.asarray(inputs["L0_attn_D_param"], dtype=f32))
    w["w_out_T"] = asc(np.asarray(inputs["L0_attn_out_proj_w"], dtype=f32).T)
    w["w_gate_in"] = asc(np.asarray(inputs["L0_ffn_w_gate"], dtype=f32))
    w["w_up_in"] = asc(np.asarray(inputs["L0_ffn_w_up"], dtype=f32))
    w["w_down_in"] = asc(np.asarray(inputs["L0_ffn_w_down"], dtype=f32))
    w["wmod_in"] = asc(np.asarray(inputs["L0_wmod"], dtype=f32))
    w["eye_in"] = np.eye(128, dtype=f32)
    w["consts_in"] = np.tile(np.array([1e-5, 1.0, 0.5, -1.0, -0.5, 65536.0,
                                       2.0, 0.25, M52, -M52, M32, -M32,
                                       M45, -M45, 65535.0, 0.00390625,
                                       256.0, 0.0, 0.0, 0.0], dtype=f32),
                             (128, 1))
    sel32 = np.zeros((32, 32 * 128), dtype=ml_dtypes.bfloat16)
    for n in range(32):
        sel32[n, n * 128:(n + 1) * 128] = 1.0
    w["sel32_in"] = sel32
    sel4 = np.zeros((4, 4 * 128), dtype=f32)
    for n in range(4):
        sel4[n, n * 128:(n + 1) * 128] = 1.0
    w["sel4_in"] = sel4
    x = asc(np.asarray(inputs["L0_attn_hidden_states"], dtype=f32))
    c = asc(np.asarray(inputs["L0_c"], dtype=f32))
    maps = []
    for b in range(NCORE):
        m = dict(w)
        m["x_in"] = asc(x[b])
        m["c_in"] = asc(c[b:b + 1])
        maps.append(m)
    return maps


def kernel(**inputs):
    L = int(np.asarray(inputs["L0_attn_hidden_states"]).shape[1])
    T = 512 if (L % 512 == 0 and L >= 2048) else max(128, L // 4)
    nc = build_nc(L, T)
    in_maps = _prep_inputs(inputs)
    res = run_bass_kernel_spmd(nc, in_maps, core_ids=list(range(NCORE)))
    out = np.stack([np.asarray(r["out"]) for r in res.results], axis=0)
    return np.ascontiguousarray(out.astype(np.float32))
